# version 07
# baseline (speedup 1.0000x reference)
import math
import numpy as np
import ml_dtypes
import concourse.bass as bass
import concourse.mybir as mybir
from concourse.bass_utils import run_bass_kernel_spmd
from contextlib import ExitStack

F32 = mybir.dt.float32
BF16 = mybir.dt.bfloat16
AF = mybir.ActivationFunctionType
OP = mybir.AluOpType
AX = mybir.AxisListType

D = 2048
DT = 16
DQ = 512
T = 4096
NCH = 8
NQ = 1024
NQT = 8
II = 5632
IT = 44
EPS = 1e-5
RSQ = math.sqrt(128.0)
NEG = -1.0e30

ENGS = ("sync", "gpsimd", "scalar", "tensor", "vector")


class Op:
    __slots__ = ("eng", "fn", "waits", "incs")

    def __init__(self, eng, fn, waits, incs):
        self.eng, self.fn, self.waits, self.incs = eng, fn, list(waits), list(incs)


class Builder:
    def __init__(self, nc, es):
        self.nc = nc
        self.es = es
        self.sems = {}
        self.cnt = {}
        for s in ("pe", "ve", "se", "ge"):
            self.sem(s)
        self.cur = None

    def sem(self, name):
        if name not in self.sems:
            self.sems[name] = self.es.enter_context(self.nc.semaphore(name))
            self.cnt[name] = 0

    def begin(self):
        self.cur = []

    def op(self, eng, fn, waits=(), incs=()):
        own = {"vector": "ve", "scalar": "se", "gpsimd": "ge"}.get(eng)
        if own is not None and fn is not None and self.cnt[own] > 0:
            waits = list(waits) + [(own, self.cnt[own])]
        for (s, k) in incs:
            self.cnt[s] += k
        self.cur.append(Op(eng, fn, waits, incs))

    def dma(self, eng, fn, sem, waits=()):
        self.sem(sem)
        self.op(eng, fn, waits, [(sem, 16)])
        return self.cnt[sem]

    def m(self, sem):
        return self.cnt.get(sem, 0)

    def bar(self):
        return dict(self.cnt)

    def bar_ops(self, marks):
        w = [(s, v) for s, v in marks.items() if v > 0]
        for e in ENGS:
            self.op(e, None, w, ())

    def emit(self):
        ops = self.cur
        self.cur = None
        with self.nc.Block() as blk:
            for eng_name in ENGS:
                eops = [o for o in ops if o.eng == eng_name]
                if not eops:
                    continue

                def body(e, eops=eops):
                    lastw = {}
                    for o in eops:
                        for (s, v) in o.waits:
                            if v > 0 and lastw.get(s, 0) < v:
                                e.wait_ge(self.sems[s], v)
                                lastw[s] = v
                        if o.fn is not None:
                            ins = o.fn(e)
                            for (s, k) in o.incs:
                                ins = ins.then_inc(self.sems[s], k)
                getattr(blk, eng_name)(body)


def build_program(dbg=False):
    nc = bass.Bass("TRN2", target_bir_lowering=False)
    _scr = "ExternalOutput" if dbg else "Internal"

    a = lambda h: h.ap()
    xq = a(nc.dram_tensor("xq", [NQ, D], F32, kind="ExternalInput"))
    xT = a(nc.dram_tensor("xT", [D, T], F32, kind="ExternalInput"))
    wq4 = a(nc.dram_tensor("wq4", [4, D, DQ], F32, kind="ExternalInput"))
    wkT = a(nc.dram_tensor("wkT", [D, DQ], F32, kind="ExternalInput"))
    wvT = a(nc.dram_tensor("wvT", [D, DQ], F32, kind="ExternalInput"))
    prj = a(nc.dram_tensor("prj", [D, D], F32, kind="ExternalInput"))
    bqd = a(nc.dram_tensor("bqd", [16, 128], F32, kind="ExternalInput"))
    bkd = a(nc.dram_tensor("bkd", [4, 128], F32, kind="ExternalInput"))
    bvd = a(nc.dram_tensor("bvd", [1, DQ], F32, kind="ExternalInput"))
    gTd = a(nc.dram_tensor("gTd", [D, II], F32, kind="ExternalInput"))
    uTd = a(nc.dram_tensor("uTd", [D, II], F32, kind="ExternalInput"))
    dTd = a(nc.dram_tensor("dTd", [II, D], F32, kind="ExternalInput"))
    gsd = a(nc.dram_tensor("gsd", [IT, 128], F32, kind="ExternalInput"))
    usd = a(nc.dram_tensor("usd", [IT, 128], F32, kind="ExternalInput"))
    dsd = a(nc.dram_tensor("dsd", [1, D], F32, kind="ExternalInput"))
    msk = a(nc.dram_tensor("msk", [NQT, 128, T], BF16, kind="ExternalInput"))
    idfd = a(nc.dram_tensor("idfd", [128, 128], F32, kind="ExternalInput"))
    onesd = a(nc.dram_tensor("onesd", [1, DQ], BF16, kind="ExternalInput"))
    outd = a(nc.dram_tensor("outd", [NQ, D], F32, kind="ExternalOutput"))

    x1fd = a(nc.dram_tensor("x1fd", [NQ, D], F32, kind=_scr))
    x1bd = a(nc.dram_tensor("x1bd", [NQ, D], BF16, kind=_scr))
    ffnd = a(nc.dram_tensor("ffnd", [NQ, D], F32, kind=_scr))

    wq4_r = wq4.rearrange("r (dt p) j -> p r dt j", p=128)
    prj_t = prj.rearrange("(tt p) o -> tt p o", p=128)
    xT_r = xT.rearrange("(dt p) t -> p dt t", p=128)
    wk_r = wkT.rearrange("(dt p) o -> p dt o", p=128)
    wv_r = wvT.rearrange("(dt p) o -> p dt o", p=128)
    gT_r = gTd.rearrange("(dt p) i -> p dt i", p=128)
    uT_r = uTd.rearrange("(dt p) i -> p dt i", p=128)
    dT_r = dTd.rearrange("(it p) o -> p it o", p=128)
    xq_r = xq.rearrange("(q p) o -> q p o", p=128)
    x1f_r = x1fd.rearrange("(q p) o -> q p o", p=128)
    x1b_r = x1bd.rearrange("(q p) o -> q p o", p=128)
    ffn_r = ffnd.rearrange("(q p) o -> q p o", p=128)
    out_r = outd.rearrange("(q p) o -> q p o", p=128)

    es = ExitStack()
    with es:
        B = Builder(nc, es)
        sb = lambda n, sh, dt: es.enter_context(nc.sbuf_tensor(n, sh, dt))
        gs_col = sb("gs_col", [128, IT], F32)
        us_col = sb("us_col", [128, IT], F32)
        onesf = sb("onesf", [1, 128], F32)
        sum8 = sb("sum8", [128, NQT], F32)
        sq8 = sb("sq8", [128, NQT], F32)
        nmax8 = sb("nmax8", [128, NQT], F32)
        rec8 = sb("rec8", [128, NQT], F32)
        nmean8 = sb("nmean8", [128, NQT], F32)
        var8 = sb("var8", [128, NQT], F32)
        rstd8 = sb("rstd8", [128, NQT], F32)
        tmp8 = sb("tmp8", [128, NQT], F32)
        l1s = sb("l1s", [128, NQT], F32)
        l1q = sb("l1q", [128, NQT], F32)
        l1nm = sb("l1nm", [128, NQT], F32)
        l1v = sb("l1v", [128, NQT], F32)
        l1t = sb("l1t", [128, NQT], F32)
        l1r = sb("l1r", [128, NQT], F32)

        esA = ExitStack()
        with esA:
            sA = lambda n, sh, dt: esA.enter_context(nc.sbuf_tensor(n, sh, dt))
            kT = sA("kT", [128, NCH, 4, 512], BF16)
            vv = sA("vv", [128, 32, DQ], BF16)
            qsT = sA("qsT", [128, 2, 4, 512], BF16)
            projsum = sA("projsum", [128, 4, D], BF16)
            idf_sb = sA("idf_sb", [128, 128], F32)
            idb_sb = sA("idb_sb", [128, 128], BF16)
            bqs_col = sA("bqs_col", [128, 4], F32)
            bk_col = sA("bk_col", [128, 4], F32)
            bv_bc = sA("bv_bc", [128, DQ], F32)

            esW = ExitStack()
            with esW:
                sW = lambda n, sh, dt: esW.enter_context(nc.sbuf_tensor(n, sh, dt))
                wqsT = sW("wqsT", [128, DT, DQ], BF16)
                wk_sb = sW("wk_sb", [128, DT, DQ], BF16)
                wv_sb = sW("wv_sb", [128, DT, DQ], BF16)
                wq4buf = [sW("wq4b0", [128, 4, DQ], BF16),
                          sW("wq4b1", [128, 4, DQ], BF16)]

                esA0 = ExitStack()
                with esA0:
                    s0 = lambda n, sh, dt: esA0.enter_context(nc.sbuf_tensor(n, sh, dt))
                    gsld = s0("gsld", [IT, 128], F32)
                    usld = s0("usld", [IT, 128], F32)
                    bqld = s0("bqld", [16, 128], F32)
                    bkld = s0("bkld", [4, 128], F32)
                    bvrow = s0("bvrow", [1, DQ], F32)
                    bqtc = s0("bqtc", [128, 16], F32)
                    ppB = esA0.enter_context(nc.psum_tensor("ppB", [128, 512], F32))

                    B.begin()
                    for (dst, src) in ((idf_sb, idfd), (gsld, gsd), (usld, usd),
                                       (bqld, bqd), (bkld, bkd), (bvrow, bvd)):
                        B.dma("sync",
                              (lambda dst, src: lambda e: e.dma_start(out=dst[:], in_=src))(dst, src),
                              "sa0")
                    mA0 = B.m("sa0")

                    B.op("vector", lambda e: e.tensor_copy(idb_sb[:], idf_sb[:]),
                         [("sa0", mA0)], [("ve", 1)])
                    B.op("vector", lambda e: e.memset(onesf[:], 1.0), (), [("ve", 1)])
                    mv_ones = B.m("ve")
                    B.op("tensor", lambda e: e.transpose(ppB[:, 0:IT], gsld[:],
                                                         idf_sb[0:IT, 0:IT]),
                         [("sa0", mA0)], [("pe", 1)])
                    mp1 = B.m("pe")
                    B.op("vector", lambda e: e.tensor_copy(gs_col[:], ppB[:, 0:IT]),
                         [("pe", mp1)], [("ve", 1)])
                    mv1 = B.m("ve")
                    B.op("tensor", lambda e: e.transpose(ppB[:, 0:IT], usld[:],
                                                         idf_sb[0:IT, 0:IT]),
                         [("ve", mv1)], [("pe", 1)])
                    mp2 = B.m("pe")
                    B.op("vector", lambda e: e.tensor_copy(us_col[:], ppB[:, 0:IT]),
                         [("pe", mp2)], [("ve", 1)])
                    mv2 = B.m("ve")
                    B.op("tensor", lambda e: e.transpose(ppB[:, 0:16], bqld[:],
                                                         idf_sb[0:16, 0:16]),
                         [("ve", mv2)], [("pe", 1)])
                    mp3 = B.m("pe")
                    B.op("vector", lambda e: e.tensor_copy(bqtc[:], ppB[:, 0:16]),
                         [("pe", mp3)], [("ve", 1)])
                    mv3 = B.m("ve")
                    B.op("tensor", lambda e: e.transpose(ppB[:, 0:4], bkld[:],
                                                         idf_sb[0:4, 0:4]),
                         [("ve", mv3)], [("pe", 1)])
                    mp4 = B.m("pe")
                    B.op("vector", lambda e: e.tensor_copy(bk_col[:], ppB[:, 0:4]),
                         [("pe", mp4)], [("ve", 1)])
                    B.op("vector", lambda e: e.tensor_add(
                        bqs_col[:], bqtc[:, 0:4], bqtc[:, 4:8]), (), [("ve", 1)])
                    B.op("vector", lambda e: e.tensor_add(
                        bqs_col[:], bqs_col[:], bqtc[:, 8:12]), (), [("ve", 1)])
                    B.op("vector", lambda e: e.tensor_add(
                        bqs_col[:], bqs_col[:], bqtc[:, 12:16]), (), [("ve", 1)])
                    mv4 = B.m("ve")
                    B.op("tensor", lambda e: e.matmul(ppB[:], onesf[:], bvrow[:],
                                                      start=True, stop=True),
                         [("ve", max(mv_ones, mv4)), ("sa0", mA0)], [("pe", 1)])
                    mp5 = B.m("pe")
                    B.op("vector", lambda e: e.tensor_copy(bv_bc[:], ppB[:]),
                         [("pe", mp5)], [("ve", 1)])
                    mv_a0 = B.m("ve")
                    mp_a0 = B.m("pe")
                    B.emit()

                esA1 = ExitStack()
                with esA1:
                    s1 = lambda n, sh, dt: esA1.enter_context(nc.sbuf_tensor(n, sh, dt))
                    xch = [s1("xch0", [128, DT, 512], BF16),
                           s1("xch1", [128, DT, 512], BF16)]
                    kps = esA1.enter_context(nc.psum_tensor("kps", [128, 4, 512], F32))
                    vps = esA1.enter_context(nc.psum_tensor("vps", [128, 4, 512], F32))

                    B.begin()
                    proc = [2, 3, 4, 5, 6, 7, 0, 1]
                    xuse = [0, 0]
                    mk_kfree = 0
                    mk_vfree = 0
                    wqf_free = [0, 0]
                    fold_dt = [0]
                    wq_done = [0]

                    mx_pre = {}
                    mx_pre[0] = B.dma("gpsimd",
                                      lambda e: e.dma_start(out=xch[0][:],
                                                            in_=xT_r[:, :, 2*512:3*512]),
                                      "sxc0", [("pe", mp_a0), ("ve", mv_a0)])
                    m_wk = B.dma("gpsimd", lambda e: e.dma_start(out=wk_sb[:], in_=wk_r),
                                 "swk")
                    mx_pre[1] = B.dma("gpsimd",
                                      lambda e: e.dma_start(out=xch[1][:],
                                                            in_=xT_r[:, :, 3*512:4*512]),
                                      "sxc1")
                    m_wv = B.dma("gpsimd", lambda e: e.dma_start(out=wv_sb[:], in_=wv_r),
                                 "swv")

                    def plan_wq_fold():
                        dt_ = fold_dt[0]
                        fold_dt[0] += 1
                        fbi = dt_ % 2
                        swq = "swq%d" % fbi
                        mwq = B.dma("gpsimd",
                                    (lambda dt_, fbi: lambda e: e.dma_start(
                                        out=wq4buf[fbi][:], in_=wq4_r[:, :, dt_, :]))(dt_, fbi),
                                    swq, [("pe", wqf_free[fbi])])
                        nonlocal_mk = mk_kfree_ref[0]
                        for r in range(4):
                            B.op("tensor",
                                 (lambda fbi, r: lambda e: e.matmul(
                                     kps[:, 0, :], idb_sb[:], wq4buf[fbi][:, r, :],
                                     start=(r == 0), stop=(r == 3)))(fbi, r),
                                 [(swq, mwq), ("ve", mv_a0), ("se", nonlocal_mk)]
                                 if r == 0 else (),
                                 [("pe", 1)] if r == 3 else ())
                        wqf_free[fbi] = B.m("pe")
                        B.op("scalar",
                             (lambda dt_: lambda e: e.activation(
                                 out=wqsT[:, dt_, :], in_=kps[:, 0, :],
                                 func=AF.Copy))(dt_),
                             [("pe", wqf_free[fbi])], [("se", 1)])
                        mk_kfree_ref[0] = B.m("se")
                        if fold_dt[0] == DT:
                            wq_done[0] = B.m("se")

                    mk_kfree_ref = [0]
                    for pi, rc in enumerate(proc):
                        bi = pi % 2
                        sxc = "sxc%d" % bi
                        if pi in mx_pre:
                            mx = mx_pre[pi]
                        else:
                            mx = B.dma("gpsimd",
                                       (lambda rc, bi: lambda e: e.dma_start(
                                           out=xch[bi][:],
                                           in_=xT_r[:, :, rc*512:(rc+1)*512]))(rc, bi),
                                       sxc,
                                       [("pe", max(xuse[bi], mp_a0)), ("ve", mv_a0)])
                        xc = xch[bi]
                        mk_kfree = mk_kfree_ref[0]
                        for dq in range(4):
                            w0 = [(sxc, mx), ("swk", m_wk), ("se", mk_kfree),
                                  ("ve", mv_a0), ("sa0", mA0)]
                            for ct in range(DT):
                                B.op("tensor",
                                     (lambda dq, ct, xc: lambda e: e.matmul(
                                         kps[:, dq, :], wk_sb[:, ct, dq*128:(dq+1)*128],
                                         xc[:, ct, :], start=(ct == 0),
                                         stop=(ct == DT-1)))(dq, ct, xc),
                                     w0 if (ct == 0 and dq == 0) else (),
                                     [("pe", 1)] if (ct == DT-1 and dq == 3) else ())
                        mp_k = B.m("pe")
                        for dq in range(4):
                            B.op("scalar",
                                 (lambda rc, dq: lambda e: e.activation(
                                     out=kT[:, rc, dq], in_=kps[:, dq, :],
                                     func=AF.Identity, bias=bk_col[:, dq:dq+1]))(rc, dq),
                                 [("pe", mp_k)], [("se", 1)])
                        mk_kfree_ref[0] = B.m("se")
                        for sv in range(4):
                            w0 = [(sxc, mx), ("swv", m_wv), ("ve", max(mk_vfree, mv_a0))]
                            for ct in range(DT):
                                B.op("tensor",
                                     (lambda sv, ct, xc: lambda e: e.matmul(
                                         vps[:, sv, :], xc[:, ct, sv*128:(sv+1)*128],
                                         wv_sb[:, ct, :], start=(ct == 0),
                                         stop=(ct == DT-1)))(sv, ct, xc),
                                     w0 if (ct == 0 and sv == 0) else (),
                                     [("pe", 1)] if (ct == DT-1 and sv == 3) else ())
                        mp_v = B.m("pe")
                        for sv in range(4):
                            B.op("vector",
                                 (lambda rc, sv: lambda e: e.scalar_tensor_tensor(
                                     out=vv[:, rc*4+sv, :], in0=vps[:, sv, :],
                                     scalar=1.0, in1=bv_bc[:],
                                     op0=OP.mult, op1=OP.add))(rc, sv),
                                 [("pe", mp_v)], [("ve", 1)])
                        mk_vfree = B.m("ve")
                        if pi < 4:
                            for _ in range(4):
                                plan_wq_fold()
                        if rc in (0, 1):
                            for dq in range(4):
                                w0 = [("se", max(mk_kfree_ref[0], wq_done[0]))]
                                for ct in range(DT):
                                    B.op("tensor",
                                         (lambda dq, ct, xc: lambda e: e.matmul(
                                             kps[:, dq, :], wqsT[:, ct, dq*128:(dq+1)*128],
                                             xc[:, ct, :], start=(ct == 0),
                                             stop=(ct == DT-1)))(dq, ct, xc),
                                         w0 if (ct == 0 and dq == 0) else (),
                                         [("pe", 1)] if (ct == DT-1 and dq == 3) else ())
                            mp_q = B.m("pe")
                            for dq in range(4):
                                B.op("scalar",
                                     (lambda rc, dq: lambda e: e.activation(
                                         out=qsT[:, rc, dq], in_=kps[:, dq, :],
                                         func=AF.Identity, bias=bqs_col[:, dq:dq+1]))(rc, dq),
                                     [("pe", mp_q)], [("se", 1)])
                            mk_kfree_ref[0] = B.m("se")
                        xuse[bi] = B.m("pe")
                    bar1 = B.bar()
                    B.emit()

            esB = ExitStack()
            with esB:
                sB = lambda n, sh, dt: esB.enter_context(nc.sbuf_tensor(n, sh, dt))
                masks = sB("masks", [128, 2, T], BF16)
                scores = sB("scores", [128, T], F32)
                Pq = sB("Pq", [128, 2, T], BF16)
                aT_sb = sB("aT_sb", [128, 2, 1024], BF16)
                zT_sb = sB("zT_sb", [128, 4, 256], BF16)
                zn_sb = sB("zn_sb", [128, 2, 512], BF16)
                xqp = sB("xqp", [128, 2, 512], F32)
                y1s = sB("y1s", [128, 2, D], F32)
                y1s16 = sB("y1s16", [128, 2, D], BF16)
                sqscr = sB("sqscr", [128, D], F32)
                scps = esB.enter_context(nc.psum_tensor("scps", [128, 2, 512], F32))
                atps = [esB.enter_context(nc.psum_tensor("atpsA", [128, 1024], BF16)),
                        esB.enter_context(nc.psum_tensor("atpsB", [128, 1024], BF16))]
                zpn = esB.enter_context(nc.psum_tensor("zpn", [128, 2, 512], F32))
                pjps = esB.enter_context(nc.psum_tensor("pjps", [128, 2, 512], F32))

                B.begin()
                B.bar_ops(bar1)
                bankmap = [scps[:, 0, :], scps[:, 1, :],
                           pjps[:, 0, :], pjps[:, 1, :]]
                pjf_free = [0, 0]
                fold_ev = [0, 0, 0, 0]
                for g_ in range(4):
                    for r_ in range(4):
                        bi_ = r_ % 2
                        spj = "spj%d" % bi_
                        mpj = B.dma("gpsimd",
                                    (lambda g_, r_, bi_: lambda e: e.dma_start(
                                        out=Pq[:, bi_, 0:D], in_=prj_t[r_ * 4 + g_]))(g_, r_, bi_),
                                    spj, [("pe", pjf_free[bi_])])
                        for c_ in range(4):
                            B.op("tensor",
                                 (lambda bi_, r_, c_: lambda e: e.matmul(
                                     bankmap[c_], idb_sb[:],
                                     Pq[:, bi_, c_*512:(c_+1)*512],
                                     start=(r_ == 0), stop=(r_ == 3)))(bi_, r_, c_),
                                 [(spj, mpj), ("ve", max(fold_ev))] if c_ == 0 else (),
                                 [("pe", 1)] if c_ == 3 else ())
                        pjf_free[bi_] = B.m("pe")
                    mp_f = B.m("pe")
                    for c_ in range(4):
                        B.op("vector",
                             (lambda g_, c_: lambda e: e.tensor_copy(
                                 projsum[:, g_, c_*512:(c_+1)*512], bankmap[c_]))(g_, c_),
                             [("pe", mp_f)], [("ve", 1)])
                        fold_ev[c_] = B.m("ve")
                mv_fold = B.m("ve")
                mp_fold = B.m("pe")

                mask_free = [0, 0]
                sc_free = [mv_fold, mv_fold]
                exp_prev = 0
                p_free = [mp_fold, mp_fold]
                atp_free = [0, 0]
                at_ready = [0, 0]
                pj_free = [mv_fold, mv_fold]
                xq_free = [0, 0]
                zevac = 0
                slot_st = [0, 0]
                zT_used = 0

                def plan_scores(g):
                    nonlocal exp_prev
                    for q in (2 * g, 2 * g + 1):
                        sl = q % 2
                        smk = "smk%d" % sl
                        mmsk = B.dma("sync",
                                     (lambda q, sl: lambda e: e.dma_start(
                                         out=masks[:, sl, :], in_=msk[q]))(q, sl),
                                     smk, [("ve", mask_free[sl])])
                        for chp in range(4):
                            for dqt in range(4):
                                for cs in range(2):
                                    ch = 2 * chp + cs
                                    B.op("tensor",
                                         (lambda q, ch, cs, dqt: lambda e: e.matmul(
                                             scps[:, cs, :],
                                             qsT[:, q//4, dqt, (q%4)*128:(q%4+1)*128],
                                             kT[:, ch, dqt, :],
                                             start=(dqt == 0), stop=(dqt == 3)))(q, ch, cs, dqt),
                                         [("ve", max(sc_free))] if (dqt == 0 and cs == 0) else (),
                                         [("pe", 1)] if (dqt == 3 and cs == 1) else ())
                            mp_sc = B.m("pe")
                            for cs in range(2):
                                ch = 2 * chp + cs
                                B.op("vector",
                                     (lambda ch, cs, sl: lambda e: e.scalar_tensor_tensor(
                                         out=scores[:, ch*512:(ch+1)*512], in0=scps[:, cs, :],
                                         scalar=1.0, in1=masks[:, sl, ch*512:(ch+1)*512],
                                         op0=OP.mult, op1=OP.add))(ch, cs, sl),
                                     [("pe", mp_sc), (smk, mmsk), ("se", exp_prev)],
                                     [("ve", 1)])
                                sc_free[cs] = B.m("ve")
                        mask_free[sl] = B.m("ve")
                        B.op("vector",
                             (lambda q: lambda e: e.tensor_reduce(
                                 nmax8[:, q:q+1], scores[:], axis=AX.X, op=OP.max,
                                 negate=True))(q),
                             (), [("ve", 1)])
                        mv_max = B.m("ve")
                        B.op("scalar",
                             (lambda q, sl: lambda e: e.activation(
                                 out=Pq[:, sl, :], in_=scores[:], func=AF.Exp,
                                 bias=nmax8[:, q:q+1], scale=1.0,
                                 accum_out=sum8[:, q:q+1]))(q, sl),
                             [("ve", mv_max), ("pe", p_free[sl])], [("se", 1)])
                        exp_prev = B.m("se")
                        B.op("vector",
                             (lambda q: lambda e: e.tensor_scalar(
                                 out=tmp8[:, q:q+1], in0=sum8[:, q:q+1], scalar1=RSQ,
                                 scalar2=None, op0=OP.mult))(q),
                             [("se", exp_prev)], [("ve", 1)])
                        B.op("vector",
                             (lambda q: lambda e: e.reciprocal(
                                 rec8[:, q:q+1], tmp8[:, q:q+1]))(q),
                             (), [("ve", 1)])

                def plan_z(g):
                    nonlocal zevac
                    for blk in range(8):
                        bk = blk % 2
                        atp = atps[bk]
                        for j in range(4):
                            st = blk * 4 + j
                            for qi in range(2):
                                B.op("tensor",
                                     (lambda st, j, qi, atp: lambda e: e.transpose(
                                         atp[:, (j*2+qi)*128:(j*2+qi+1)*128],
                                         Pq[:, qi, st*128:(st+1)*128], idb_sb[:]))(st, j, qi, atp),
                                     [("ve", atp_free[bk]), ("se", exp_prev)]
                                     if (j == 0 and qi == 0) else (),
                                     [("pe", 1)] if (j == 3 and qi == 1) else ())
                        mp_tr = B.m("pe")
                        B.op("vector",
                             (lambda bk, atp: lambda e: e.tensor_copy(
                                 aT_sb[:, bk, :], atp[:]))(bk, atp),
                             [("pe", mp_tr)], [("ve", 1)])
                        at_ready[bk] = B.m("ve")
                        atp_free[bk] = at_ready[bk]
                        for j in range(4):
                            st = blk * 4 + j
                            for tq in range(2):
                                B.op("tensor",
                                     (lambda st, bk, j, tq: lambda e: e.matmul(
                                         zpn[:, tq, :],
                                         aT_sb[:, bk, (j*2+tq)*128:(j*2+tq+1)*128],
                                         vv[:, st, :], start=(st == 0),
                                         stop=(st == 31)))(st, bk, j, tq),
                                     [("ve", max(at_ready[bk], zevac))]
                                     if (j == 0 and tq == 0) else (),
                                     [("pe", 1)] if (st == 31 and tq == 1) else ())
                    p_free[0] = p_free[1] = B.m("pe")
                    mp_z = B.m("pe")
                    B.op("vector", lambda e: e.tensor_copy(zn_sb[:], zpn[:]),
                         [("pe", mp_z)], [("ve", 1)])
                    mv_zn = B.m("ve")
                    for tq in range(2):
                        for dv in range(4):
                            B.op("tensor",
                                 (lambda tq, dv: lambda e: e.transpose(
                                     atps[tq][:, dv*128:(dv+1)*128],
                                     zn_sb[:, tq, dv*128:(dv+1)*128], idb_sb[:]))(tq, dv),
                                 [("ve", mv_zn), ("pe", zT_used)] if dv == 0 else (),
                                 [("pe", 1)] if dv == 3 else ())
                        mp_zt = B.m("pe")
                        B.op("vector",
                             (lambda tq: lambda e: e.tensor_copy(
                                 zT_sb[:, :, tq*128:(tq+1)*128],
                                 atps[tq][:, 0:512].rearrange("p (dv t) -> p dv t", dv=4)))(tq),
                             [("pe", mp_zt)], [("ve", 1)])
                    zevac = B.m("ve")
                    atp_free[0] = atp_free[1] = zevac

                def plan_proj_ln1(g):
                    nonlocal zT_used
                    for tq in range(2):
                        q = 2 * g + tq
                        sx = "sx%d" % tq
                        for dc in range(4):
                            ps = dc % 2
                            sxq = "sxq%d" % ps
                            mxq = B.dma("sync",
                                        (lambda q, dc, ps: lambda e: e.dma_start(
                                            out=xqp[:, ps, :],
                                            in_=xq_r[q][:, dc*512:(dc+1)*512]))(q, dc, ps),
                                        sxq, [("ve", xq_free[ps])])
                            for dv in range(4):
                                B.op("tensor",
                                     (lambda tq, dc, ps, dv: lambda e: e.matmul(
                                         pjps[:, ps, :], zT_sb[:, dv, tq*128:(tq+1)*128],
                                         projsum[:, dv, dc*512:(dc+1)*512],
                                         start=(dv == 0), stop=(dv == 3)))(tq, dc, ps, dv),
                                     [("ve", max(zevac, pj_free[ps]))] if dv == 0 else (),
                                     [("pe", 1)] if dv == 3 else ())
                            mp_pj = B.m("pe")
                            B.op("vector",
                                 (lambda q, tq, dc, ps: lambda e: e.scalar_tensor_tensor(
                                     out=y1s[:, tq, dc*512:(dc+1)*512], in0=pjps[:, ps, :],
                                     scalar=rec8[:, q:q+1], in1=xqp[:, ps, :],
                                     op0=OP.mult, op1=OP.add))(q, tq, dc, ps),
                                 [("pe", mp_pj), (sxq, mxq), (sx, slot_st[tq])],
                                 [("ve", 1)])
                            pj_free[ps] = B.m("ve")
                            xq_free[ps] = B.m("ve")
                        mv_y1 = B.m("ve")
                    zT_used = B.m("pe")

                def plan_ln1(g):
                    for tq in range(2):
                        q = 2 * g + tq
                        sx = "sx%d" % tq
                        mv_ydone = B.m("ve")
                        B.op("vector",
                             (lambda q, tq: lambda e: e.tensor_reduce(
                                 l1s[:, q:q+1], y1s[:, tq, :], axis=AX.X, op=OP.add))(q, tq),
                             (), [("ve", 1)])
                        B.op("scalar",
                             (lambda q, tq: lambda e: e.activation(
                                 out=sqscr[:], in_=y1s[:, tq, :], func=AF.Square,
                                 accum_out=l1q[:, q:q+1]))(q, tq),
                             [("ve", mv_ydone)], [("se", 1)])
                        ms_sq = B.m("se")
                        B.op("gpsimd",
                             (lambda q: lambda e: e.tensor_scalar(
                                 out=l1nm[:, q:q+1], in0=l1s[:, q:q+1],
                                 scalar1=-1.0/D, scalar2=None, op0=OP.mult))(q),
                             [("ve", B.m("ve"))], [("ge", 1)])
                        B.op("gpsimd",
                             (lambda q: lambda e: e.tensor_mul(
                                 l1t[:, q:q+1], l1nm[:, q:q+1], l1nm[:, q:q+1]))(q),
                             (), [("ge", 1)])
                        B.op("gpsimd",
                             (lambda q: lambda e: e.tensor_scalar(
                                 out=l1v[:, q:q+1], in0=l1q[:, q:q+1],
                                 scalar1=1.0/D, scalar2=None, op0=OP.mult))(q),
                             [("se", ms_sq)], [("ge", 1)])
                        B.op("gpsimd",
                             (lambda q: lambda e: e.tensor_sub(
                                 l1v[:, q:q+1], l1v[:, q:q+1], l1t[:, q:q+1]))(q),
                             (), [("ge", 1)])
                        B.op("gpsimd",
                             (lambda q: lambda e: e.tensor_scalar(
                                 out=l1v[:, q:q+1], in0=l1v[:, q:q+1],
                                 scalar1=float(EPS), scalar2=None, op0=OP.add))(q),
                             (), [("ge", 1)])
                        mg_var = B.m("ge")
                        B.op("scalar",
                             (lambda q: lambda e: e.activation(
                                 out=l1t[:, q:q+1], in_=l1v[:, q:q+1], func=AF.Ln))(q),
                             [("ge", mg_var)], [("se", 1)])
                        B.op("scalar",
                             (lambda q: lambda e: e.activation(
                                 out=l1r[:, q:q+1], in_=l1t[:, q:q+1], func=AF.Exp,
                                 scale=-0.5))(q),
                             (), [("se", 1)])
                        ms_rs = B.m("se")
                        B.op("gpsimd",
                             (lambda q, tq: lambda e: e.tensor_scalar(
                                 out=y1s[:, tq, :], in0=y1s[:, tq, :],
                                 scalar1=l1nm[:, q:q+1], scalar2=l1r[:, q:q+1],
                                 op0=OP.add, op1=OP.mult))(q, tq),
                             [("se", ms_rs)], [("ge", 1)])
                        B.op("gpsimd",
                             (lambda tq: lambda e: e.tensor_copy(
                                 y1s16[:, tq, :], y1s[:, tq, :]))(tq),
                             (), [("ge", 1)])
                        mg_a = B.m("ge")
                        B.dma("sync",
                              (lambda q, tq: lambda e: e.dma_start(
                                  out=x1f_r[q], in_=y1s[:, tq, :]))(q, tq),
                              sx, [("ge", mg_a)])
                        B.dma("sync",
                              (lambda q, tq: lambda e: e.dma_start(
                                  out=x1b_r[q], in_=y1s16[:, tq, :]))(q, tq),
                              sx)
                        slot_st[tq] = B.m(sx)

                pending = None
                for g in range(4):
                    plan_scores(g)
                    if pending is not None:
                        plan_proj_ln1(pending)
                    plan_z(g)
                    if pending is not None:
                        plan_ln1(pending)
                    pending = g
                plan_proj_ln1(pending)
                plan_ln1(pending)
                bar2 = B.bar()
                B.emit()

        esD = ExitStack()
        with esD:
            sD = lambda n, sh, dt: esD.enter_context(nc.sbuf_tensor(n, sh, dt))
            hT = sD("hT", [128, IT, NQ], BF16)

            esD1 = ExitStack()
            with esD1:
                s1 = lambda n, sh, dt: esD1.enter_context(nc.sbuf_tensor(n, sh, dt))
                x1T = s1("x1T", [128, DT, NQ], BF16)
                gstr = [s1("gstr0", [128, DT, 128], BF16),
                        s1("gstr1", [128, DT, 128], BF16)]
                ustr = [s1("ustr0", [128, DT, 128], BF16),
                        s1("ustr1", [128, DT, 128], BF16)]
                silu_sb = [s1("silu0", [128, 512], F32),
                           s1("silu1", [128, 512], F32)]
                gups = esD1.enter_context(nc.psum_tensor("gups", [128, 4, 512], F32))

                B.begin()
                B.bar_ops(bar2)
                for dt_ in range(DT):
                    B.dma("sync",
                          (lambda dt_: lambda e: e.dma_start(
                              out=x1T[:, dt_, :], in_=x1bd[:, dt_*128:(dt_+1)*128],
                              transpose=True))(dt_),
                          "sx1T")
                mx1T = B.m("sx1T")
                str_free = [0, 0]
                bank_free = [0, 0, 0, 0]
                silu_free = [0, 0]
                for it in range(IT):
                    bi = it % 2
                    sgs = "sgs%d" % bi
                    B.dma("gpsimd",
                          (lambda it, bi: lambda e: e.dma_start(
                              out=gstr[bi][:], in_=gT_r[:, :, it*128:(it+1)*128]))(it, bi),
                          sgs, [("pe", str_free[bi])])
                    mstr = B.dma("gpsimd",
                                 (lambda it, bi: lambda e: e.dma_start(
                                     out=ustr[bi][:], in_=uT_r[:, :, it*128:(it+1)*128]))(it, bi),
                                 sgs)
                    for ct in range(DT):
                        for tch in range(2):
                            B.op("tensor",
                                 (lambda bi, tch, ct: lambda e: e.matmul(
                                     gups[:, tch*2, :], gstr[bi][:, ct, :],
                                     x1T[:, ct, tch*512:(tch+1)*512],
                                     start=(ct == 0), stop=(ct == DT-1)))(bi, tch, ct),
                                 [(sgs, mstr), ("sx1T", mx1T),
                                  ("ve", max(bank_free[0], bank_free[2]))]
                                 if (ct == 0 and tch == 0) else (),
                                 [("pe", 1)] if (ct == DT-1 and tch == 1) else ())
                    mp_g = B.m("pe")
                    for ct in range(DT):
                        for tch in range(2):
                            B.op("tensor",
                                 (lambda bi, tch, ct: lambda e: e.matmul(
                                     gups[:, tch*2+1, :], ustr[bi][:, ct, :],
                                     x1T[:, ct, tch*512:(tch+1)*512],
                                     start=(ct == 0), stop=(ct == DT-1)))(bi, tch, ct),
                                 [("ve", max(bank_free[1], bank_free[3]))]
                                 if (ct == 0 and tch == 0) else (),
                                 [("pe", 1)] if (ct == DT-1 and tch == 1) else ())
                    mp_u = B.m("pe")
                    for tch in range(2):
                        ssl = tch
                        B.op("scalar",
                             (lambda it, tch, ssl: lambda e: e.activation(
                                 out=silu_sb[ssl][:], in_=gups[:, tch*2, :], func=AF.Silu,
                                 scale=gs_col[:, it:it+1]))(it, tch, ssl),
                             [("pe", mp_g), ("ve", silu_free[ssl])], [("se", 1)])
                        ms_sl = B.m("se")
                        B.op("vector",
                             (lambda it, tch, ssl: lambda e: e.scalar_tensor_tensor(
                                 out=hT[:, it, tch*512:(tch+1)*512], in0=gups[:, tch*2+1, :],
                                 scalar=us_col[:, it:it+1], in1=silu_sb[ssl][:],
                                 op0=OP.mult, op1=OP.mult))(it, tch, ssl),
                             [("pe", mp_u), ("se", ms_sl)], [("ve", 1)])
                        mvh = B.m("ve")
                        silu_free[ssl] = mvh
                        bank_free[tch*2] = mvh
                        bank_free[tch*2+1] = mvh
                    str_free[bi] = B.m("pe")
                bar4 = B.bar()
                B.emit()

            esE = ExitStack()
            with esE:
                sE = lambda n, sh, dt: esE.enter_context(nc.sbuf_tensor(n, sh, dt))
                dstr = [sE("dstr0", [128, IT, 256], BF16),
                        sE("dstr1", [128, IT, 256], BF16),
                        sE("dstr2", [128, IT, 256], BF16)]
                ffns = sE("ffns", [128, 2, 256], F32)
                ds_b = sE("ds_b", [128, D], F32)
                dsrow = sE("dsrow", [1, D], F32)
                dnps = [esE.enter_context(nc.psum_tensor("dnps0", [128, 512], F32)),
                        esE.enter_context(nc.psum_tensor("dnps1", [128, 512], F32))]
                dspp = esE.enter_context(nc.psum_tensor("dspp", [128, 512], F32))

                B.begin()
                B.bar_ops(bar4)
                m_ds = B.dma("sync", lambda e: e.dma_start(out=dsrow[:], in_=dsd), "sds")
                pp_free = 0
                for k in range(4):
                    B.op("tensor",
                         (lambda k: lambda e: e.matmul(
                             dspp[:], onesf[:], dsrow[0:1, k*512:(k+1)*512],
                             start=True, stop=True))(k),
                         [("sds", m_ds), ("ve", pp_free)], [("pe", 1)])
                    mpk = B.m("pe")
                    B.op("vector",
                         (lambda k: lambda e: e.tensor_copy(
                             ds_b[:, k*512:(k+1)*512], dspp[:]))(k),
                         [("pe", mpk)], [("ve", 1)])
                    pp_free = B.m("ve")
                mv_dsb = B.m("ve")
                dstr_free = [0, 0, 0]
                dn_free = [0, 0]
                fst = [0, 0]
                md_m = [0, 0, 0]
                for dc8 in range(8):
                    bi = dc8 % 3
                    sd = "sd%d" % bi
                    md_m[bi] = B.dma("gpsimd",
                                     (lambda dc8, bi: lambda e: e.dma_start(
                                         out=dstr[bi][:],
                                         in_=dT_r[:, :, dc8*256:(dc8+1)*256]))(dc8, bi),
                                     sd, [("pe", dstr_free[bi])])
                    if dc8 % 2 == 1:
                        dca, dcb = dc8 - 1, dc8
                        ba, bb = dca % 3, dcb % 3
                        for ts in range(8):
                            for it in range(IT):
                                B.op("tensor",
                                     (lambda ba, ts, it: lambda e: e.matmul(
                                         dnps[0][:, 0:256], hT[:, it, ts*128:(ts+1)*128],
                                         dstr[ba][:, it, :],
                                         start=(it == 0), stop=(it == IT-1)))(ba, ts, it),
                                     [("sd%d" % ba, md_m[ba]), ("sd%d" % bb, md_m[bb]),
                                      ("ve", max(dn_free))] if it == 0 else ())
                                B.op("tensor",
                                     (lambda bb, ts, it: lambda e: e.matmul(
                                         dnps[1][:, 0:256], hT[:, it, ts*128:(ts+1)*128],
                                         dstr[bb][:, it, :],
                                         start=(it == 0), stop=(it == IT-1)))(bb, ts, it),
                                     (),
                                     [("pe", 1)] if it == IT-1 else ())
                            mp_d = B.m("pe")
                            for k, dcx in enumerate((dca, dcb)):
                                sff = "sff%d" % k
                                B.op("vector",
                                     (lambda dcx, k: lambda e: e.scalar_tensor_tensor(
                                         out=ffns[:, k, :], in0=dnps[k][:, 0:256], scalar=1.0,
                                         in1=ds_b[:, dcx*256:(dcx+1)*256],
                                         op0=OP.mult, op1=OP.mult))(dcx, k),
                                     [("pe", mp_d), (sff, fst[k]), ("ve", mv_dsb)],
                                     [("ve", 1)])
                                mv_f = B.m("ve")
                                dn_free[k] = mv_f
                                B.dma("sync",
                                      (lambda dcx, ts, k: lambda e: e.dma_start(
                                          out=ffn_r[ts][:, dcx*256:(dcx+1)*256],
                                          in_=ffns[:, k, :]))(dcx, ts, k),
                                      sff, [("ve", mv_f)])
                                fst[k] = B.m(sff)
                        dstr_free[ba] = B.m("pe")
                        dstr_free[bb] = B.m("pe")
                bar5 = B.bar()
                B.emit()

        esF = ExitStack()
        with esF:
            sF = lambda n, sh, dt: esF.enter_context(nc.sbuf_tensor(n, sh, dt))
            y2t = sF("y2t", [128, NQT, D], F32)
            rx = sF("rx", [128, 2, D], F32)
            rf = sF("rf", [128, 2, D], F32)
            outs = sF("outs", [128, 2, D], F32)
            sqscr2 = sF("sqscr2", [128, D], F32)

            B.begin()
            B.bar_ops(bar5)
            r_free = [0, 0]
            for q in range(NQT):
                sl = q % 2
                sfr = "sfr%d" % sl
                B.dma("sync",
                      (lambda q, sl: lambda e: e.dma_start(out=rx[:, sl, :], in_=x1f_r[q]))(q, sl),
                      sfr, [("ve", r_free[sl])])
                m2 = B.dma("sync",
                           (lambda q, sl: lambda e: e.dma_start(out=rf[:, sl, :], in_=ffn_r[q]))(q, sl),
                           sfr)
                B.op("vector",
                     (lambda q, sl: lambda e: e.tensor_add(
                         y2t[:, q, :], rx[:, sl, :], rf[:, sl, :]))(q, sl),
                     [(sfr, m2)], [("ve", 1)])
                r_free[sl] = B.m("ve")
                B.op("vector",
                     (lambda q: lambda e: e.tensor_reduce(
                         sum8[:, q:q+1], y2t[:, q, :], axis=AX.X, op=OP.add))(q),
                     (), [("ve", 1)])
                mv_add = B.m("ve")
                B.op("scalar",
                     (lambda q: lambda e: e.activation(
                         out=sqscr2[:], in_=y2t[:, q, :], func=AF.Square,
                         accum_out=sq8[:, q:q+1]))(q),
                     [("ve", mv_add)], [("se", 1)])
            mv_s, ms_s = B.m("ve"), B.m("se")
            B.op("vector", lambda e: e.tensor_scalar(
                out=nmean8[:], in0=sum8[:], scalar1=-1.0/D, scalar2=None, op0=OP.mult),
                [("ve", mv_s)], [("ve", 1)])
            B.op("vector", lambda e: e.tensor_mul(tmp8[:], nmean8[:], nmean8[:]),
                 (), [("ve", 1)])
            B.op("vector", lambda e: e.tensor_scalar(
                out=var8[:], in0=sq8[:], scalar1=1.0/D, scalar2=None, op0=OP.mult),
                [("se", ms_s)], [("ve", 1)])
            B.op("vector", lambda e: e.tensor_sub(var8[:], var8[:], tmp8[:]),
                 (), [("ve", 1)])
            B.op("vector", lambda e: e.tensor_scalar(
                out=var8[:], in0=var8[:], scalar1=float(EPS), scalar2=None, op0=OP.add),
                (), [("ve", 1)])
            mv_v = B.m("ve")
            B.op("scalar", lambda e: e.activation(
                out=tmp8[:], in_=var8[:], func=AF.Ln),
                [("ve", mv_v)], [("se", 1)])
            B.op("scalar", lambda e: e.activation(
                out=rstd8[:], in_=tmp8[:], func=AF.Exp, scale=-0.5),
                (), [("se", 1)])
            ms_v = B.m("se")
            o_free = [0, 0]
            for q in range(NQT):
                sl = q % 2
                so = "so%d" % sl
                B.sem(so)
                B.op("vector",
                     (lambda q, sl: lambda e: e.tensor_scalar(
                         out=outs[:, sl, :], in0=y2t[:, q, :],
                         scalar1=nmean8[:, q:q+1], scalar2=rstd8[:, q:q+1],
                         op0=OP.add, op1=OP.mult))(q, sl),
                     [("se", ms_v), (so, o_free[sl])], [("ve", 1)])
                mv_o = B.m("ve")
                B.dma("sync",
                      (lambda q, sl: lambda e: e.dma_start(out=out_r[q], in_=outs[:, sl, :]))(q, sl),
                      so, [("ve", mv_o)])
                o_free[sl] = B.m(so)
            B.op("sync", None, [("so0", B.m("so0")), ("so1", B.m("so1"))], ())
            B.emit()

    return nc


_bf = ml_dtypes.bfloat16


def kernel(L0_attn_x, L0_attn_WQ, L0_attn_bQ, L0_attn_WK, L0_attn_bK,
           L0_attn_WV, L0_attn_bV, L0_attn_proj, L0_ffn_gate_w, L0_ffn_up_w,
           L0_ffn_down_w, L0_ffn_gate_s, L0_ffn_up_s, L0_ffn_down_s):
    f32 = np.float32
    x = np.asarray(L0_attn_x, f32)
    _wqT = np.asarray(L0_attn_WQ, f32).T
    wq4 = np.ascontiguousarray(
        np.stack([_wqT[:, r*DQ:(r+1)*DQ] for r in range(4)], axis=0))
    wkT = np.ascontiguousarray(np.asarray(L0_attn_WK, f32).T)
    wvT = np.ascontiguousarray(np.asarray(L0_attn_WV, f32).T)
    prj = np.ascontiguousarray(np.asarray(L0_attn_proj, f32))
    gT = np.ascontiguousarray(np.asarray(L0_ffn_gate_w, f32).T)
    uT = np.ascontiguousarray(np.asarray(L0_ffn_up_w, f32).T)
    dT = np.ascontiguousarray(np.asarray(L0_ffn_down_w, f32).T)
    bq = np.ascontiguousarray(np.asarray(L0_attn_bQ, f32).reshape(16, 128))
    bk = np.ascontiguousarray(np.asarray(L0_attn_bK, f32).reshape(4, 128))
    bv = np.ascontiguousarray(np.asarray(L0_attn_bV, f32).reshape(1, DQ))
    gs = np.ascontiguousarray(np.asarray(L0_ffn_gate_s, f32).reshape(IT, 128))
    us = np.ascontiguousarray(np.asarray(L0_ffn_up_s, f32).reshape(IT, 128))
    ds = np.ascontiguousarray(np.asarray(L0_ffn_down_s, f32).reshape(1, D))
    idf = np.eye(128, dtype=f32)
    ones2 = np.ones((1, DQ), dtype=_bf)

    in_maps = []
    for c in range(8):
        b, base = c // 4, (c % 4) * NQ
        xb = x[b]
        xqc = np.ascontiguousarray(xb[base:base + NQ])
        xTc = np.ascontiguousarray(np.roll(xb, -base, axis=0).T)
        tg = base + np.arange(NQ, dtype=np.int64)[:, None]
        sg = ((base + np.arange(T, dtype=np.int64)) % T)[None, :]
        maskc = np.where(sg > tg, f32(NEG), f32(0.0)).astype(_bf).reshape(NQT, 128, T)
        in_maps.append(dict(
            xq=xqc, xT=xTc, wq4=wq4, wkT=wkT, wvT=wvT, prj=prj, bqd=bq,
            bkd=bk, bvd=bv, gTd=gT, uTd=uT, dTd=dT, gsd=gs, usd=us, dsd=ds,
            msk=maskc, idfd=idf, onesd=ones2))

    nc = build_program()
    res = run_bass_kernel_spmd(nc, in_maps, core_ids=list(range(8)))
    outp = np.empty((2, T, D), dtype=f32)
    for c in range(8):
        b, base = c // 4, (c % 4) * NQ
        outp[b, base:base + NQ] = np.asarray(res.results[c]["outd"], dtype=f32)
    return outp


# version 08
# speedup vs baseline: 1.1797x; 1.1797x over previous
import math
import numpy as np
import ml_dtypes
import concourse.bass as bass
import concourse.mybir as mybir
from concourse.bass_utils import run_bass_kernel_spmd
from contextlib import ExitStack

F32 = mybir.dt.float32
BF16 = mybir.dt.bfloat16
AF = mybir.ActivationFunctionType
OP = mybir.AluOpType
AX = mybir.AxisListType

D = 2048
DT = 16
DQ = 512
T = 4096
NCH = 8
NQ = 1024
NQT = 8
II = 5632
IT = 44
EPS = 1e-5
RSQ = math.sqrt(128.0)
NEG = -1.0e30

ENGS = ("sync", "gpsimd", "scalar", "tensor", "vector")


class Op:
    __slots__ = ("eng", "fn", "waits", "incs")

    def __init__(self, eng, fn, waits, incs):
        self.eng, self.fn, self.waits, self.incs = eng, fn, list(waits), list(incs)


class Builder:
    def __init__(self, nc, es):
        self.nc = nc
        self.es = es
        self.sems = {}
        self.cnt = {}
        for s in ("pe", "ve", "se", "ge"):
            self.sem(s)
        self.cur = None

    def sem(self, name):
        if name not in self.sems:
            self.sems[name] = self.es.enter_context(self.nc.semaphore(name))
            self.cnt[name] = 0

    def begin(self):
        self.cur = []

    def op(self, eng, fn, waits=(), incs=()):
        own = {"vector": "ve", "scalar": "se", "gpsimd": "ge"}.get(eng)
        if own is not None and fn is not None and self.cnt[own] > 0:
            waits = list(waits) + [(own, self.cnt[own])]
        for (s, k) in incs:
            self.cnt[s] += k
        self.cur.append(Op(eng, fn, waits, incs))

    def dma(self, eng, fn, sem, waits=()):
        self.sem(sem)
        self.op(eng, fn, waits, [(sem, 16)])
        return self.cnt[sem]

    def m(self, sem):
        return self.cnt.get(sem, 0)

    def bar(self):
        return dict(self.cnt)

    def bar_ops(self, marks):
        w = [(s, v) for s, v in marks.items() if v > 0]
        for e in ENGS:
            self.op(e, None, w, ())

    def emit(self):
        ops = self.cur
        self.cur = None
        with self.nc.Block() as blk:
            for eng_name in ENGS:
                eops = [o for o in ops if o.eng == eng_name]
                if not eops:
                    continue

                def body(e, eops=eops):
                    lastw = {}
                    for o in eops:
                        for (s, v) in o.waits:
                            if v > 0 and lastw.get(s, 0) < v:
                                e.wait_ge(self.sems[s], v)
                                lastw[s] = v
                        if o.fn is not None:
                            ins = o.fn(e)
                            for (s, k) in o.incs:
                                ins = ins.then_inc(self.sems[s], k)
                getattr(blk, eng_name)(body)


def build_program(dbg=False):
    nc = bass.Bass("TRN2", target_bir_lowering=False)
    _scr = "ExternalOutput" if dbg else "Internal"

    a = lambda h: h.ap()
    xq = a(nc.dram_tensor("xq", [NQ, D], F32, kind="ExternalInput"))
    xT = a(nc.dram_tensor("xT", [D, T], F32, kind="ExternalInput"))
    wq4 = a(nc.dram_tensor("wq4", [4, D, DQ], F32, kind="ExternalInput"))
    wkT = a(nc.dram_tensor("wkT", [D, DQ], F32, kind="ExternalInput"))
    wvT = a(nc.dram_tensor("wvT", [D, DQ], F32, kind="ExternalInput"))
    prj = a(nc.dram_tensor("prj", [D, D], F32, kind="ExternalInput"))
    bqd = a(nc.dram_tensor("bqd", [16, 128], F32, kind="ExternalInput"))
    bkd = a(nc.dram_tensor("bkd", [4, 128], F32, kind="ExternalInput"))
    bvd = a(nc.dram_tensor("bvd", [1, DQ], F32, kind="ExternalInput"))
    gTd = a(nc.dram_tensor("gTd", [D, II], F32, kind="ExternalInput"))
    uTd = a(nc.dram_tensor("uTd", [D, II], F32, kind="ExternalInput"))
    dTd = a(nc.dram_tensor("dTd", [II, D], F32, kind="ExternalInput"))
    gsd = a(nc.dram_tensor("gsd", [IT, 128], F32, kind="ExternalInput"))
    usd = a(nc.dram_tensor("usd", [IT, 128], F32, kind="ExternalInput"))
    dsd = a(nc.dram_tensor("dsd", [1, D], F32, kind="ExternalInput"))
    msk = a(nc.dram_tensor("msk", [NQT, 128, T], BF16, kind="ExternalInput"))
    idfd = a(nc.dram_tensor("idfd", [128, 128], F32, kind="ExternalInput"))
    onesd = a(nc.dram_tensor("onesd", [1, DQ], BF16, kind="ExternalInput"))
    outd = a(nc.dram_tensor("outd", [NQ, D], F32, kind="ExternalOutput"))

    x1fd = a(nc.dram_tensor("x1fd", [NQ, D], F32, kind=_scr))
    x1bd = a(nc.dram_tensor("x1bd", [NQ, D], BF16, kind=_scr))
    ffnd = a(nc.dram_tensor("ffnd", [NQ, D], F32, kind=_scr))

    wq4_r = wq4.rearrange("r (dt p) j -> p r dt j", p=128)
    prj_t = prj.rearrange("(tt p) o -> tt p o", p=128)
    xT_r = xT.rearrange("(dt p) t -> p dt t", p=128)
    wk_r = wkT.rearrange("(dt p) o -> p dt o", p=128)
    wv_r = wvT.rearrange("(dt p) o -> p dt o", p=128)
    gT_r = gTd.rearrange("(dt p) i -> p dt i", p=128)
    uT_r = uTd.rearrange("(dt p) i -> p dt i", p=128)
    dT_r = dTd.rearrange("(it p) o -> p it o", p=128)
    xq_r = xq.rearrange("(q p) o -> q p o", p=128)
    x1f_r = x1fd.rearrange("(q p) o -> q p o", p=128)
    x1b_r = x1bd.rearrange("(q p) o -> q p o", p=128)
    ffn_r = ffnd.rearrange("(q p) o -> q p o", p=128)
    out_r = outd.rearrange("(q p) o -> q p o", p=128)

    es = ExitStack()
    with es:
        B = Builder(nc, es)
        sb = lambda n, sh, dt: es.enter_context(nc.sbuf_tensor(n, sh, dt))
        gs_col = sb("gs_col", [128, IT], F32)
        us_col = sb("us_col", [128, IT], F32)
        onesf = sb("onesf", [1, 128], F32)
        sum8 = sb("sum8", [128, NQT], F32)
        sq8 = sb("sq8", [128, NQT], F32)
        nmax8 = sb("nmax8", [128, NQT], F32)
        rec8 = sb("rec8", [128, NQT], F32)
        nmean8 = sb("nmean8", [128, NQT], F32)
        var8 = sb("var8", [128, NQT], F32)
        rstd8 = sb("rstd8", [128, NQT], F32)
        tmp8 = sb("tmp8", [128, NQT], F32)
        l1s = sb("l1s", [128, NQT], F32)
        l1q = sb("l1q", [128, NQT], F32)
        l1nm = sb("l1nm", [128, NQT], F32)
        l1v = sb("l1v", [128, NQT], F32)
        l1t = sb("l1t", [128, NQT], F32)
        l1r = sb("l1r", [128, NQT], F32)

        esA = ExitStack()
        with esA:
            sA = lambda n, sh, dt: esA.enter_context(nc.sbuf_tensor(n, sh, dt))
            kT = sA("kT", [128, NCH, 4, 512], BF16)
            vv = sA("vv", [128, 32, DQ], BF16)
            qsT = sA("qsT", [128, 2, 4, 512], BF16)
            projsum = sA("projsum", [128, 4, D], BF16)
            idf_sb = sA("idf_sb", [128, 128], F32)
            idb_sb = sA("idb_sb", [128, 128], BF16)
            bqs_col = sA("bqs_col", [128, 4], F32)
            bk_col = sA("bk_col", [128, 4], F32)
            bv_bc = sA("bv_bc", [128, DQ], F32)

            esW = ExitStack()
            with esW:
                sW = lambda n, sh, dt: esW.enter_context(nc.sbuf_tensor(n, sh, dt))
                wqsT = sW("wqsT", [128, DT, DQ], BF16)
                wk_sb = sW("wk_sb", [128, DT, DQ], BF16)
                wv_sb = sW("wv_sb", [128, DT, DQ], BF16)
                wq4buf = [sW("wq4b0", [128, 4, DQ], BF16),
                          sW("wq4b1", [128, 4, DQ], BF16)]

                esA0 = ExitStack()
                with esA0:
                    s0 = lambda n, sh, dt: esA0.enter_context(nc.sbuf_tensor(n, sh, dt))
                    gsld = s0("gsld", [IT, 128], F32)
                    usld = s0("usld", [IT, 128], F32)
                    bqld = s0("bqld", [16, 128], F32)
                    bkld = s0("bkld", [4, 128], F32)
                    bvrow = s0("bvrow", [1, DQ], F32)
                    bqtc = s0("bqtc", [128, 16], F32)
                    ppB = esA0.enter_context(nc.psum_tensor("ppB", [128, 512], F32))

                    B.begin()
                    for (dst, src) in ((idf_sb, idfd), (gsld, gsd), (usld, usd),
                                       (bqld, bqd), (bkld, bkd), (bvrow, bvd)):
                        B.dma("sync",
                              (lambda dst, src: lambda e: e.dma_start(out=dst[:], in_=src))(dst, src),
                              "sa0")
                    mA0 = B.m("sa0")

                    B.op("vector", lambda e: e.tensor_copy(idb_sb[:], idf_sb[:]),
                         [("sa0", mA0)], [("ve", 1)])
                    B.op("vector", lambda e: e.memset(onesf[:], 1.0), (), [("ve", 1)])
                    mv_ones = B.m("ve")
                    B.op("tensor", lambda e: e.transpose(ppB[:, 0:IT], gsld[:],
                                                         idf_sb[0:IT, 0:IT]),
                         [("sa0", mA0)], [("pe", 1)])
                    mp1 = B.m("pe")
                    B.op("vector", lambda e: e.tensor_copy(gs_col[:], ppB[:, 0:IT]),
                         [("pe", mp1)], [("ve", 1)])
                    mv1 = B.m("ve")
                    B.op("tensor", lambda e: e.transpose(ppB[:, 0:IT], usld[:],
                                                         idf_sb[0:IT, 0:IT]),
                         [("ve", mv1)], [("pe", 1)])
                    mp2 = B.m("pe")
                    B.op("vector", lambda e: e.tensor_copy(us_col[:], ppB[:, 0:IT]),
                         [("pe", mp2)], [("ve", 1)])
                    mv2 = B.m("ve")
                    B.op("tensor", lambda e: e.transpose(ppB[:, 0:16], bqld[:],
                                                         idf_sb[0:16, 0:16]),
                         [("ve", mv2)], [("pe", 1)])
                    mp3 = B.m("pe")
                    B.op("vector", lambda e: e.tensor_copy(bqtc[:], ppB[:, 0:16]),
                         [("pe", mp3)], [("ve", 1)])
                    mv3 = B.m("ve")
                    B.op("tensor", lambda e: e.transpose(ppB[:, 0:4], bkld[:],
                                                         idf_sb[0:4, 0:4]),
                         [("ve", mv3)], [("pe", 1)])
                    mp4 = B.m("pe")
                    B.op("vector", lambda e: e.tensor_copy(bk_col[:], ppB[:, 0:4]),
                         [("pe", mp4)], [("ve", 1)])
                    B.op("vector", lambda e: e.tensor_add(
                        bqs_col[:], bqtc[:, 0:4], bqtc[:, 4:8]), (), [("ve", 1)])
                    B.op("vector", lambda e: e.tensor_add(
                        bqs_col[:], bqs_col[:], bqtc[:, 8:12]), (), [("ve", 1)])
                    B.op("vector", lambda e: e.tensor_add(
                        bqs_col[:], bqs_col[:], bqtc[:, 12:16]), (), [("ve", 1)])
                    mv4 = B.m("ve")
                    B.op("tensor", lambda e: e.matmul(ppB[:], onesf[:], bvrow[:],
                                                      start=True, stop=True),
                         [("ve", max(mv_ones, mv4)), ("sa0", mA0)], [("pe", 1)])
                    mp5 = B.m("pe")
                    B.op("vector", lambda e: e.tensor_copy(bv_bc[:], ppB[:]),
                         [("pe", mp5)], [("ve", 1)])
                    mv_a0 = B.m("ve")
                    mp_a0 = B.m("pe")
                    B.emit()

                esA1 = ExitStack()
                with esA1:
                    s1 = lambda n, sh, dt: esA1.enter_context(nc.sbuf_tensor(n, sh, dt))
                    xch = [s1("xch0", [128, DT, 512], BF16),
                           s1("xch1", [128, DT, 512], BF16)]
                    kps = esA1.enter_context(nc.psum_tensor("kps", [128, 4, 512], F32))
                    vps = esA1.enter_context(nc.psum_tensor("vps", [128, 4, 512], F32))

                    B.begin()
                    proc = [2, 3, 4, 5, 6, 7, 0, 1]
                    xuse = [0, 0]
                    mk_kfree = 0
                    mk_vfree = 0
                    wqf_free = [0, 0]
                    fold_dt = [0]
                    wq_done = [0]

                    mx_pre = {}
                    mx_pre[0] = B.dma("gpsimd",
                                      lambda e: e.dma_start(out=xch[0][:],
                                                            in_=xT_r[:, :, 2*512:3*512]),
                                      "sxc0", [("pe", mp_a0), ("ve", mv_a0)])
                    m_wk = B.dma("gpsimd", lambda e: e.dma_start(out=wk_sb[:], in_=wk_r),
                                 "swk")
                    mx_pre[1] = B.dma("gpsimd",
                                      lambda e: e.dma_start(out=xch[1][:],
                                                            in_=xT_r[:, :, 3*512:4*512]),
                                      "sxc1")
                    m_wv = B.dma("gpsimd", lambda e: e.dma_start(out=wv_sb[:], in_=wv_r),
                                 "swv")

                    def plan_wq_fold():
                        dt_ = fold_dt[0]
                        fold_dt[0] += 1
                        fbi = dt_ % 2
                        swq = "swq%d" % fbi
                        mwq = B.dma("gpsimd",
                                    (lambda dt_, fbi: lambda e: e.dma_start(
                                        out=wq4buf[fbi][:], in_=wq4_r[:, :, dt_, :]))(dt_, fbi),
                                    swq, [("pe", wqf_free[fbi])])
                        nonlocal_mk = mk_kfree_ref[0]
                        for r in range(4):
                            B.op("tensor",
                                 (lambda fbi, r: lambda e: e.matmul(
                                     kps[:, 0, :], idb_sb[:], wq4buf[fbi][:, r, :],
                                     start=(r == 0), stop=(r == 3)))(fbi, r),
                                 [(swq, mwq), ("ve", mv_a0), ("se", nonlocal_mk)]
                                 if r == 0 else (),
                                 [("pe", 1)] if r == 3 else ())
                        wqf_free[fbi] = B.m("pe")
                        B.op("scalar",
                             (lambda dt_: lambda e: e.activation(
                                 out=wqsT[:, dt_, :], in_=kps[:, 0, :],
                                 func=AF.Copy))(dt_),
                             [("pe", wqf_free[fbi])], [("se", 1)])
                        mk_kfree_ref[0] = B.m("se")
                        if fold_dt[0] == DT:
                            wq_done[0] = B.m("se")

                    mk_kfree_ref = [0]
                    for pi, rc in enumerate(proc):
                        bi = pi % 2
                        sxc = "sxc%d" % bi
                        if pi in mx_pre:
                            mx = mx_pre[pi]
                        else:
                            mx = B.dma("gpsimd",
                                       (lambda rc, bi: lambda e: e.dma_start(
                                           out=xch[bi][:],
                                           in_=xT_r[:, :, rc*512:(rc+1)*512]))(rc, bi),
                                       sxc,
                                       [("pe", max(xuse[bi], mp_a0)), ("ve", mv_a0)])
                        xc = xch[bi]
                        mk_kfree = mk_kfree_ref[0]
                        for dq in range(4):
                            w0 = [(sxc, mx), ("swk", m_wk), ("se", mk_kfree),
                                  ("ve", mv_a0), ("sa0", mA0)]
                            for ct in range(DT):
                                B.op("tensor",
                                     (lambda dq, ct, xc: lambda e: e.matmul(
                                         kps[:, dq, :], wk_sb[:, ct, dq*128:(dq+1)*128],
                                         xc[:, ct, :], start=(ct == 0),
                                         stop=(ct == DT-1)))(dq, ct, xc),
                                     w0 if (ct == 0 and dq == 0) else (),
                                     [("pe", 1)] if (ct == DT-1 and dq == 3) else ())
                        mp_k = B.m("pe")
                        for dq in range(4):
                            B.op("scalar",
                                 (lambda rc, dq: lambda e: e.activation(
                                     out=kT[:, rc, dq], in_=kps[:, dq, :],
                                     func=AF.Identity, bias=bk_col[:, dq:dq+1]))(rc, dq),
                                 [("pe", mp_k)], [("se", 1)])
                        mk_kfree_ref[0] = B.m("se")
                        for sv in range(4):
                            w0 = [(sxc, mx), ("swv", m_wv), ("ve", max(mk_vfree, mv_a0))]
                            for ct in range(DT):
                                B.op("tensor",
                                     (lambda sv, ct, xc: lambda e: e.matmul(
                                         vps[:, sv, :], xc[:, ct, sv*128:(sv+1)*128],
                                         wv_sb[:, ct, :], start=(ct == 0),
                                         stop=(ct == DT-1)))(sv, ct, xc),
                                     w0 if (ct == 0 and sv == 0) else (),
                                     [("pe", 1)] if (ct == DT-1 and sv == 3) else ())
                        mp_v = B.m("pe")
                        for sv in range(4):
                            B.op("vector",
                                 (lambda rc, sv: lambda e: e.scalar_tensor_tensor(
                                     out=vv[:, rc*4+sv, :], in0=vps[:, sv, :],
                                     scalar=1.0, in1=bv_bc[:],
                                     op0=OP.mult, op1=OP.add))(rc, sv),
                                 [("pe", mp_v)], [("ve", 1)])
                        mk_vfree = B.m("ve")
                        if pi < 4:
                            for _ in range(4):
                                plan_wq_fold()
                        if rc in (0, 1):
                            for dq in range(4):
                                w0 = [("se", max(mk_kfree_ref[0], wq_done[0]))]
                                for ct in range(DT):
                                    B.op("tensor",
                                         (lambda dq, ct, xc: lambda e: e.matmul(
                                             kps[:, dq, :], wqsT[:, ct, dq*128:(dq+1)*128],
                                             xc[:, ct, :], start=(ct == 0),
                                             stop=(ct == DT-1)))(dq, ct, xc),
                                         w0 if (ct == 0 and dq == 0) else (),
                                         [("pe", 1)] if (ct == DT-1 and dq == 3) else ())
                            mp_q = B.m("pe")
                            for dq in range(4):
                                B.op("scalar",
                                     (lambda rc, dq: lambda e: e.activation(
                                         out=qsT[:, rc, dq], in_=kps[:, dq, :],
                                         func=AF.Identity, bias=bqs_col[:, dq:dq+1]))(rc, dq),
                                     [("pe", mp_q)], [("se", 1)])
                            mk_kfree_ref[0] = B.m("se")
                        xuse[bi] = B.m("pe")
                    bar1 = B.bar()
                    B.emit()

            esB = ExitStack()
            with esB:
                sB = lambda n, sh, dt: esB.enter_context(nc.sbuf_tensor(n, sh, dt))
                masks = sB("masks", [128, 2, T], BF16)
                scores = sB("scores", [128, T], F32)
                Pq = sB("Pq", [128, 2, T], BF16)
                aT_sb = sB("aT_sb", [128, 2, 1024], BF16)
                zT_sb = sB("zT_sb", [128, 4, 256], BF16)
                zn_sb = sB("zn_sb", [128, 2, 512], BF16)
                xqp = sB("xqp", [128, 2, 512], F32)
                y1s = sB("y1s", [128, 2, D], F32)
                y1s16 = sB("y1s16", [128, 2, D], BF16)
                sqscr = sB("sqscr", [128, D], F32)
                scps = esB.enter_context(nc.psum_tensor("scps", [128, 2, 512], F32))
                atps = [esB.enter_context(nc.psum_tensor("atpsA", [128, 1024], BF16)),
                        esB.enter_context(nc.psum_tensor("atpsB", [128, 1024], BF16))]
                zpn = esB.enter_context(nc.psum_tensor("zpn", [128, 2, 512], F32))
                pjps = esB.enter_context(nc.psum_tensor("pjps", [128, 2, 512], F32))

                B.begin()
                B.bar_ops(bar1)
                bankmap = [scps[:, 0, :], scps[:, 1, :],
                           pjps[:, 0, :], pjps[:, 1, :]]
                pjf_free = [0, 0]
                fold_ev = [0, 0, 0, 0]
                for g_ in range(4):
                    for r_ in range(4):
                        bi_ = r_ % 2
                        spj = "spj%d" % bi_
                        mpj = B.dma("gpsimd",
                                    (lambda g_, r_, bi_: lambda e: e.dma_start(
                                        out=Pq[:, bi_, 0:D], in_=prj_t[r_ * 4 + g_]))(g_, r_, bi_),
                                    spj, [("pe", pjf_free[bi_])])
                        for c_ in range(4):
                            B.op("tensor",
                                 (lambda bi_, r_, c_: lambda e: e.matmul(
                                     bankmap[c_], idb_sb[:],
                                     Pq[:, bi_, c_*512:(c_+1)*512],
                                     start=(r_ == 0), stop=(r_ == 3)))(bi_, r_, c_),
                                 [(spj, mpj), ("ve", max(fold_ev))] if c_ == 0 else (),
                                 [("pe", 1)] if c_ == 3 else ())
                        pjf_free[bi_] = B.m("pe")
                    mp_f = B.m("pe")
                    for c_ in range(4):
                        B.op("vector",
                             (lambda g_, c_: lambda e: e.tensor_copy(
                                 projsum[:, g_, c_*512:(c_+1)*512], bankmap[c_]))(g_, c_),
                             [("pe", mp_f)], [("ve", 1)])
                        fold_ev[c_] = B.m("ve")
                mv_fold = B.m("ve")
                mp_fold = B.m("pe")

                mask_free = [0, 0]
                sc_free = [mv_fold, mv_fold]
                exp_prev = 0
                p_free = [mp_fold, mp_fold]
                atp_free = [0, 0]
                at_ready = [0, 0]
                pj_free = [mv_fold, mv_fold]
                xq_free = [0, 0]
                zevac = 0
                slot_st = [0, 0]
                zT_used = 0

                def plan_scores(g):
                    nonlocal exp_prev
                    for q in (2 * g, 2 * g + 1):
                        sl = q % 2
                        smk = "smk%d" % sl
                        mmsk = B.dma("sync",
                                     (lambda q, sl: lambda e: e.dma_start(
                                         out=masks[:, sl, :], in_=msk[q]))(q, sl),
                                     smk, [("ve", mask_free[sl])])
                        for ch in range(NCH):
                            cs = ch % 2
                            for dqt in range(4):
                                B.op("tensor",
                                     (lambda q, ch, cs, dqt: lambda e: e.matmul(
                                         scps[:, cs, :],
                                         qsT[:, q//4, dqt, (q%4)*128:(q%4+1)*128],
                                         kT[:, ch, dqt, :],
                                         start=(dqt == 0), stop=(dqt == 3)))(q, ch, cs, dqt),
                                     [("ve", sc_free[cs])] if dqt == 0 else (),
                                     [("pe", 1)] if dqt == 3 else ())
                            mp_sc = B.m("pe")
                            B.op("vector",
                                 (lambda ch, cs, sl: lambda e: e.scalar_tensor_tensor(
                                     out=scores[:, ch*512:(ch+1)*512], in0=scps[:, cs, :],
                                     scalar=1.0, in1=masks[:, sl, ch*512:(ch+1)*512],
                                     op0=OP.mult, op1=OP.add))(ch, cs, sl),
                                 [("pe", mp_sc), (smk, mmsk), ("se", exp_prev)],
                                 [("ve", 1)])
                            sc_free[cs] = B.m("ve")
                        mask_free[sl] = B.m("ve")
                        B.op("vector",
                             (lambda q: lambda e: e.tensor_reduce(
                                 nmax8[:, q:q+1], scores[:], axis=AX.X, op=OP.max,
                                 negate=True))(q),
                             (), [("ve", 1)])
                        mv_max = B.m("ve")
                        B.op("scalar",
                             (lambda q, sl: lambda e: e.activation(
                                 out=Pq[:, sl, :], in_=scores[:], func=AF.Exp,
                                 bias=nmax8[:, q:q+1], scale=1.0,
                                 accum_out=sum8[:, q:q+1]))(q, sl),
                             [("ve", mv_max), ("pe", p_free[sl])], [("se", 1)])
                        exp_prev = B.m("se")
                        B.op("vector",
                             (lambda q: lambda e: e.tensor_scalar(
                                 out=tmp8[:, q:q+1], in0=sum8[:, q:q+1], scalar1=RSQ,
                                 scalar2=None, op0=OP.mult))(q),
                             [("se", exp_prev)], [("ve", 1)])
                        B.op("vector",
                             (lambda q: lambda e: e.reciprocal(
                                 rec8[:, q:q+1], tmp8[:, q:q+1]))(q),
                             (), [("ve", 1)])

                def plan_z(g):
                    nonlocal zevac
                    for blk in range(8):
                        bk = blk % 2
                        atp = atps[bk]
                        for j in range(4):
                            st = blk * 4 + j
                            for qi in range(2):
                                B.op("tensor",
                                     (lambda st, j, qi, atp: lambda e: e.transpose(
                                         atp[:, (j*2+qi)*128:(j*2+qi+1)*128],
                                         Pq[:, qi, st*128:(st+1)*128], idb_sb[:]))(st, j, qi, atp),
                                     [("ve", atp_free[bk]), ("se", exp_prev)]
                                     if (j == 0 and qi == 0) else (),
                                     [("pe", 1)] if (j == 3 and qi == 1) else ())
                        mp_tr = B.m("pe")
                        B.op("vector",
                             (lambda bk, atp: lambda e: e.tensor_copy(
                                 aT_sb[:, bk, :], atp[:]))(bk, atp),
                             [("pe", mp_tr)], [("ve", 1)])
                        at_ready[bk] = B.m("ve")
                        atp_free[bk] = at_ready[bk]
                        for j in range(4):
                            st = blk * 4 + j
                            for tq in range(2):
                                B.op("tensor",
                                     (lambda st, bk, j, tq: lambda e: e.matmul(
                                         zpn[:, tq, :],
                                         aT_sb[:, bk, (j*2+tq)*128:(j*2+tq+1)*128],
                                         vv[:, st, :], start=(st == 0),
                                         stop=(st == 31)))(st, bk, j, tq),
                                     [("ve", max(at_ready[bk], zevac))]
                                     if (j == 0 and tq == 0) else (),
                                     [("pe", 1)] if (st == 31 and tq == 1) else ())
                    p_free[0] = p_free[1] = B.m("pe")
                    mp_z = B.m("pe")
                    B.op("vector", lambda e: e.tensor_copy(zn_sb[:], zpn[:]),
                         [("pe", mp_z)], [("ve", 1)])
                    mv_zn = B.m("ve")
                    for tq in range(2):
                        for dv in range(4):
                            B.op("tensor",
                                 (lambda tq, dv: lambda e: e.transpose(
                                     atps[tq][:, dv*128:(dv+1)*128],
                                     zn_sb[:, tq, dv*128:(dv+1)*128], idb_sb[:]))(tq, dv),
                                 [("ve", mv_zn), ("pe", zT_used)] if dv == 0 else (),
                                 [("pe", 1)] if dv == 3 else ())
                        mp_zt = B.m("pe")
                        B.op("vector",
                             (lambda tq: lambda e: e.tensor_copy(
                                 zT_sb[:, :, tq*128:(tq+1)*128],
                                 atps[tq][:, 0:512].rearrange("p (dv t) -> p dv t", dv=4)))(tq),
                             [("pe", mp_zt)], [("ve", 1)])
                    zevac = B.m("ve")
                    atp_free[0] = atp_free[1] = zevac

                def plan_proj_ln1(g):
                    nonlocal zT_used
                    for tq in range(2):
                        q = 2 * g + tq
                        sx = "sx%d" % tq
                        for dc in range(4):
                            ps = dc % 2
                            sxq = "sxq%d" % ps
                            mxq = B.dma("sync",
                                        (lambda q, dc, ps: lambda e: e.dma_start(
                                            out=xqp[:, ps, :],
                                            in_=xq_r[q][:, dc*512:(dc+1)*512]))(q, dc, ps),
                                        sxq, [("ve", xq_free[ps])])
                            for dv in range(4):
                                B.op("tensor",
                                     (lambda tq, dc, ps, dv: lambda e: e.matmul(
                                         pjps[:, ps, :], zT_sb[:, dv, tq*128:(tq+1)*128],
                                         projsum[:, dv, dc*512:(dc+1)*512],
                                         start=(dv == 0), stop=(dv == 3)))(tq, dc, ps, dv),
                                     [("ve", max(zevac, pj_free[ps]))] if dv == 0 else (),
                                     [("pe", 1)] if dv == 3 else ())
                            mp_pj = B.m("pe")
                            B.op("vector",
                                 (lambda q, tq, dc, ps: lambda e: e.scalar_tensor_tensor(
                                     out=y1s[:, tq, dc*512:(dc+1)*512], in0=pjps[:, ps, :],
                                     scalar=rec8[:, q:q+1], in1=xqp[:, ps, :],
                                     op0=OP.mult, op1=OP.add))(q, tq, dc, ps),
                                 [("pe", mp_pj), (sxq, mxq), (sx, slot_st[tq])],
                                 [("ve", 1)])
                            pj_free[ps] = B.m("ve")
                            xq_free[ps] = B.m("ve")
                        mv_y1 = B.m("ve")
                    zT_used = B.m("pe")

                def plan_ln1(g):
                    for tq in range(2):
                        q = 2 * g + tq
                        sx = "sx%d" % tq
                        mv_ydone = B.m("ve")
                        B.op("vector",
                             (lambda q, tq: lambda e: e.tensor_reduce(
                                 l1s[:, q:q+1], y1s[:, tq, :], axis=AX.X, op=OP.add))(q, tq),
                             (), [("ve", 1)])
                        B.op("scalar",
                             (lambda q, tq: lambda e: e.activation(
                                 out=sqscr[:], in_=y1s[:, tq, :], func=AF.Square,
                                 accum_out=l1q[:, q:q+1]))(q, tq),
                             [("ve", mv_ydone)], [("se", 1)])
                        ms_sq = B.m("se")
                        B.op("gpsimd",
                             (lambda q: lambda e: e.tensor_scalar(
                                 out=l1nm[:, q:q+1], in0=l1s[:, q:q+1],
                                 scalar1=-1.0/D, scalar2=None, op0=OP.mult))(q),
                             [("ve", B.m("ve"))], [("ge", 1)])
                        B.op("gpsimd",
                             (lambda q: lambda e: e.tensor_mul(
                                 l1t[:, q:q+1], l1nm[:, q:q+1], l1nm[:, q:q+1]))(q),
                             (), [("ge", 1)])
                        B.op("gpsimd",
                             (lambda q: lambda e: e.tensor_scalar(
                                 out=l1v[:, q:q+1], in0=l1q[:, q:q+1],
                                 scalar1=1.0/D, scalar2=None, op0=OP.mult))(q),
                             [("se", ms_sq)], [("ge", 1)])
                        B.op("gpsimd",
                             (lambda q: lambda e: e.tensor_sub(
                                 l1v[:, q:q+1], l1v[:, q:q+1], l1t[:, q:q+1]))(q),
                             (), [("ge", 1)])
                        B.op("gpsimd",
                             (lambda q: lambda e: e.tensor_scalar(
                                 out=l1v[:, q:q+1], in0=l1v[:, q:q+1],
                                 scalar1=float(EPS), scalar2=None, op0=OP.add))(q),
                             (), [("ge", 1)])
                        mg_var = B.m("ge")
                        B.op("scalar",
                             (lambda q: lambda e: e.activation(
                                 out=l1t[:, q:q+1], in_=l1v[:, q:q+1], func=AF.Ln))(q),
                             [("ge", mg_var)], [("se", 1)])
                        B.op("scalar",
                             (lambda q: lambda e: e.activation(
                                 out=l1r[:, q:q+1], in_=l1t[:, q:q+1], func=AF.Exp,
                                 scale=-0.5))(q),
                             (), [("se", 1)])
                        ms_rs = B.m("se")
                        B.op("gpsimd",
                             (lambda q, tq: lambda e: e.tensor_scalar(
                                 out=y1s[:, tq, :], in0=y1s[:, tq, :],
                                 scalar1=l1nm[:, q:q+1], scalar2=l1r[:, q:q+1],
                                 op0=OP.add, op1=OP.mult))(q, tq),
                             [("se", ms_rs)], [("ge", 1)])
                        B.op("gpsimd",
                             (lambda tq: lambda e: e.tensor_copy(
                                 y1s16[:, tq, :], y1s[:, tq, :]))(tq),
                             (), [("ge", 1)])
                        mg_a = B.m("ge")
                        B.dma("sync",
                              (lambda q, tq: lambda e: e.dma_start(
                                  out=x1f_r[q], in_=y1s[:, tq, :]))(q, tq),
                              sx, [("ge", mg_a)])
                        B.dma("sync",
                              (lambda q, tq: lambda e: e.dma_start(
                                  out=x1b_r[q], in_=y1s16[:, tq, :]))(q, tq),
                              sx)
                        slot_st[tq] = B.m(sx)

                pending = None
                for g in range(4):
                    plan_scores(g)
                    if pending is not None:
                        plan_proj_ln1(pending)
                    plan_z(g)
                    if pending is not None:
                        plan_ln1(pending)
                    pending = g
                plan_proj_ln1(pending)
                plan_ln1(pending)
                bar2 = B.bar()
                B.emit()

        esD = ExitStack()
        with esD:
            sD = lambda n, sh, dt: esD.enter_context(nc.sbuf_tensor(n, sh, dt))
            hT = sD("hT", [128, IT, NQ], BF16)

            esD1 = ExitStack()
            with esD1:
                s1 = lambda n, sh, dt: esD1.enter_context(nc.sbuf_tensor(n, sh, dt))
                x1T = s1("x1T", [128, DT, NQ], BF16)
                gstr = [s1("gstr0", [128, DT, 128], BF16),
                        s1("gstr1", [128, DT, 128], BF16)]
                ustr = [s1("ustr0", [128, DT, 128], BF16),
                        s1("ustr1", [128, DT, 128], BF16)]
                silu_sb = [s1("silu0", [128, 512], F32),
                           s1("silu1", [128, 512], F32)]
                gups = esD1.enter_context(nc.psum_tensor("gups", [128, 4, 512], F32))

                B.begin()
                B.bar_ops(bar2)
                for dt_ in range(DT):
                    B.dma("sync",
                          (lambda dt_: lambda e: e.dma_start(
                              out=x1T[:, dt_, :], in_=x1bd[:, dt_*128:(dt_+1)*128],
                              transpose=True))(dt_),
                          "sx1T")
                mx1T = B.m("sx1T")
                str_free = [0, 0]
                bank_free = [0, 0, 0, 0]
                silu_free = [0, 0]
                for it in range(IT):
                    bi = it % 2
                    sgs = "sgs%d" % bi
                    B.dma("gpsimd",
                          (lambda it, bi: lambda e: e.dma_start(
                              out=gstr[bi][:], in_=gT_r[:, :, it*128:(it+1)*128]))(it, bi),
                          sgs, [("pe", str_free[bi])])
                    mstr = B.dma("gpsimd",
                                 (lambda it, bi: lambda e: e.dma_start(
                                     out=ustr[bi][:], in_=uT_r[:, :, it*128:(it+1)*128]))(it, bi),
                                 sgs)
                    for tch in range(2):
                        gsl, usl = tch * 2, tch * 2 + 1
                        for ct in range(DT):
                            B.op("tensor",
                                 (lambda bi, tch, gsl, ct: lambda e: e.matmul(
                                     gups[:, gsl, :], gstr[bi][:, ct, :],
                                     x1T[:, ct, tch*512:(tch+1)*512],
                                     start=(ct == 0), stop=(ct == DT-1)))(bi, tch, gsl, ct),
                                 [(sgs, mstr), ("sx1T", mx1T), ("ve", bank_free[gsl])]
                                 if ct == 0 else (),
                                 [("pe", 1)] if ct == DT-1 else ())
                        mp_g = B.m("pe")
                        for ct in range(DT):
                            B.op("tensor",
                                 (lambda bi, tch, usl, ct: lambda e: e.matmul(
                                     gups[:, usl, :], ustr[bi][:, ct, :],
                                     x1T[:, ct, tch*512:(tch+1)*512],
                                     start=(ct == 0), stop=(ct == DT-1)))(bi, tch, usl, ct),
                                 [("ve", bank_free[usl])] if ct == 0 else (),
                                 [("pe", 1)] if ct == DT-1 else ())
                        mp_u = B.m("pe")
                        ssl = tch
                        B.op("scalar",
                             (lambda it, gsl, ssl: lambda e: e.activation(
                                 out=silu_sb[ssl][:], in_=gups[:, gsl, :], func=AF.Silu,
                                 scale=gs_col[:, it:it+1]))(it, gsl, ssl),
                             [("pe", mp_g), ("ve", silu_free[ssl])], [("se", 1)])
                        ms_sl = B.m("se")
                        B.op("vector",
                             (lambda it, tch, usl, ssl: lambda e: e.scalar_tensor_tensor(
                                 out=hT[:, it, tch*512:(tch+1)*512], in0=gups[:, usl, :],
                                 scalar=us_col[:, it:it+1], in1=silu_sb[ssl][:],
                                 op0=OP.mult, op1=OP.mult))(it, tch, usl, ssl),
                             [("pe", mp_u), ("se", ms_sl)], [("ve", 1)])
                        mvh = B.m("ve")
                        silu_free[ssl] = mvh
                        bank_free[usl] = mvh
                        bank_free[gsl] = mvh
                    str_free[bi] = B.m("pe")
                bar4 = B.bar()
                B.emit()

            esE = ExitStack()
            with esE:
                sE = lambda n, sh, dt: esE.enter_context(nc.sbuf_tensor(n, sh, dt))
                dstr = [sE("dstr0", [128, IT, 256], BF16),
                        sE("dstr1", [128, IT, 256], BF16),
                        sE("dstr2", [128, IT, 256], BF16)]
                ffns = sE("ffns", [128, 2, 256], F32)
                ds_b = sE("ds_b", [128, D], F32)
                dsrow = sE("dsrow", [1, D], F32)
                dnps = [esE.enter_context(nc.psum_tensor("dnps0", [128, 512], F32)),
                        esE.enter_context(nc.psum_tensor("dnps1", [128, 512], F32))]
                dspp = esE.enter_context(nc.psum_tensor("dspp", [128, 512], F32))

                B.begin()
                B.bar_ops(bar4)
                m_ds = B.dma("sync", lambda e: e.dma_start(out=dsrow[:], in_=dsd), "sds")
                pp_free = 0
                for k in range(4):
                    B.op("tensor",
                         (lambda k: lambda e: e.matmul(
                             dspp[:], onesf[:], dsrow[0:1, k*512:(k+1)*512],
                             start=True, stop=True))(k),
                         [("sds", m_ds), ("ve", pp_free)], [("pe", 1)])
                    mpk = B.m("pe")
                    B.op("vector",
                         (lambda k: lambda e: e.tensor_copy(
                             ds_b[:, k*512:(k+1)*512], dspp[:]))(k),
                         [("pe", mpk)], [("ve", 1)])
                    pp_free = B.m("ve")
                mv_dsb = B.m("ve")
                dstr_free = [0, 0, 0]
                dn_free = [0, 0]
                fst = [0, 0]
                for dc8 in range(8):
                    bi = dc8 % 3
                    sd = "sd%d" % bi
                    md = B.dma("gpsimd",
                               (lambda dc8, bi: lambda e: e.dma_start(
                                   out=dstr[bi][:], in_=dT_r[:, :, dc8*256:(dc8+1)*256]))(dc8, bi),
                               sd, [("pe", dstr_free[bi])])
                    for ts in range(8):
                        ps = (dc8 * 8 + ts) % 2
                        sff = "sff%d" % ps
                        for it in range(IT):
                            B.op("tensor",
                                 (lambda bi, ts, ps, it: lambda e: e.matmul(
                                     dnps[ps][:, 0:256], hT[:, it, ts*128:(ts+1)*128],
                                     dstr[bi][:, it, :],
                                     start=(it == 0), stop=(it == IT-1)))(bi, ts, ps, it),
                                 [(sd, md), ("ve", dn_free[ps])] if it == 0 else (),
                                 [("pe", 1)] if it == IT-1 else ())
                        mp_d = B.m("pe")
                        B.op("vector",
                             (lambda dc8, ps: lambda e: e.scalar_tensor_tensor(
                                 out=ffns[:, ps, :], in0=dnps[ps][:, 0:256], scalar=1.0,
                                 in1=ds_b[:, dc8*256:(dc8+1)*256],
                                 op0=OP.mult, op1=OP.mult))(dc8, ps),
                             [("pe", mp_d), (sff, fst[ps]), ("ve", mv_dsb)],
                             [("ve", 1)])
                        mv_f = B.m("ve")
                        dn_free[ps] = mv_f
                        B.dma("sync",
                              (lambda dc8, ts, ps: lambda e: e.dma_start(
                                  out=ffn_r[ts][:, dc8*256:(dc8+1)*256],
                                  in_=ffns[:, ps, :]))(dc8, ts, ps),
                              sff, [("ve", mv_f)])
                        fst[ps] = B.m(sff)
                    dstr_free[bi] = B.m("pe")
                bar5 = B.bar()
                B.emit()

        esF = ExitStack()
        with esF:
            sF = lambda n, sh, dt: esF.enter_context(nc.sbuf_tensor(n, sh, dt))
            y2t = sF("y2t", [128, NQT, D], F32)
            rx = sF("rx", [128, 2, D], F32)
            rf = sF("rf", [128, 2, D], F32)
            outs = sF("outs", [128, 2, D], F32)
            sqscr2 = sF("sqscr2", [128, D], F32)

            B.begin()
            B.bar_ops(bar5)
            r_free = [0, 0]
            for q in range(NQT):
                sl = q % 2
                sfr = "sfr%d" % sl
                B.dma("sync",
                      (lambda q, sl: lambda e: e.dma_start(out=rx[:, sl, :], in_=x1f_r[q]))(q, sl),
                      sfr, [("ve", r_free[sl])])
                m2 = B.dma("sync",
                           (lambda q, sl: lambda e: e.dma_start(out=rf[:, sl, :], in_=ffn_r[q]))(q, sl),
                           sfr)
                B.op("vector",
                     (lambda q, sl: lambda e: e.tensor_add(
                         y2t[:, q, :], rx[:, sl, :], rf[:, sl, :]))(q, sl),
                     [(sfr, m2)], [("ve", 1)])
                r_free[sl] = B.m("ve")
                B.op("vector",
                     (lambda q: lambda e: e.tensor_reduce(
                         sum8[:, q:q+1], y2t[:, q, :], axis=AX.X, op=OP.add))(q),
                     (), [("ve", 1)])
                mv_add = B.m("ve")
                B.op("scalar",
                     (lambda q: lambda e: e.activation(
                         out=sqscr2[:], in_=y2t[:, q, :], func=AF.Square,
                         accum_out=sq8[:, q:q+1]))(q),
                     [("ve", mv_add)], [("se", 1)])
            mv_s, ms_s = B.m("ve"), B.m("se")
            B.op("vector", lambda e: e.tensor_scalar(
                out=nmean8[:], in0=sum8[:], scalar1=-1.0/D, scalar2=None, op0=OP.mult),
                [("ve", mv_s)], [("ve", 1)])
            B.op("vector", lambda e: e.tensor_mul(tmp8[:], nmean8[:], nmean8[:]),
                 (), [("ve", 1)])
            B.op("vector", lambda e: e.tensor_scalar(
                out=var8[:], in0=sq8[:], scalar1=1.0/D, scalar2=None, op0=OP.mult),
                [("se", ms_s)], [("ve", 1)])
            B.op("vector", lambda e: e.tensor_sub(var8[:], var8[:], tmp8[:]),
                 (), [("ve", 1)])
            B.op("vector", lambda e: e.tensor_scalar(
                out=var8[:], in0=var8[:], scalar1=float(EPS), scalar2=None, op0=OP.add),
                (), [("ve", 1)])
            mv_v = B.m("ve")
            B.op("scalar", lambda e: e.activation(
                out=tmp8[:], in_=var8[:], func=AF.Ln),
                [("ve", mv_v)], [("se", 1)])
            B.op("scalar", lambda e: e.activation(
                out=rstd8[:], in_=tmp8[:], func=AF.Exp, scale=-0.5),
                (), [("se", 1)])
            ms_v = B.m("se")
            o_free = [0, 0]
            for q in range(NQT):
                sl = q % 2
                so = "so%d" % sl
                B.sem(so)
                B.op("vector",
                     (lambda q, sl: lambda e: e.tensor_scalar(
                         out=outs[:, sl, :], in0=y2t[:, q, :],
                         scalar1=nmean8[:, q:q+1], scalar2=rstd8[:, q:q+1],
                         op0=OP.add, op1=OP.mult))(q, sl),
                     [("se", ms_v), (so, o_free[sl])], [("ve", 1)])
                mv_o = B.m("ve")
                B.dma("sync",
                      (lambda q, sl: lambda e: e.dma_start(out=out_r[q], in_=outs[:, sl, :]))(q, sl),
                      so, [("ve", mv_o)])
                o_free[sl] = B.m(so)
            B.op("sync", None, [("so0", B.m("so0")), ("so1", B.m("so1"))], ())
            B.emit()

    return nc


_bf = ml_dtypes.bfloat16


def kernel(L0_attn_x, L0_attn_WQ, L0_attn_bQ, L0_attn_WK, L0_attn_bK,
           L0_attn_WV, L0_attn_bV, L0_attn_proj, L0_ffn_gate_w, L0_ffn_up_w,
           L0_ffn_down_w, L0_ffn_gate_s, L0_ffn_up_s, L0_ffn_down_s):
    f32 = np.float32
    x = np.asarray(L0_attn_x, f32)
    _wqT = np.asarray(L0_attn_WQ, f32).T
    wq4 = np.ascontiguousarray(
        np.stack([_wqT[:, r*DQ:(r+1)*DQ] for r in range(4)], axis=0))
    wkT = np.ascontiguousarray(np.asarray(L0_attn_WK, f32).T)
    wvT = np.ascontiguousarray(np.asarray(L0_attn_WV, f32).T)
    prj = np.ascontiguousarray(np.asarray(L0_attn_proj, f32))
    gT = np.ascontiguousarray(np.asarray(L0_ffn_gate_w, f32).T)
    uT = np.ascontiguousarray(np.asarray(L0_ffn_up_w, f32).T)
    dT = np.ascontiguousarray(np.asarray(L0_ffn_down_w, f32).T)
    bq = np.ascontiguousarray(np.asarray(L0_attn_bQ, f32).reshape(16, 128))
    bk = np.ascontiguousarray(np.asarray(L0_attn_bK, f32).reshape(4, 128))
    bv = np.ascontiguousarray(np.asarray(L0_attn_bV, f32).reshape(1, DQ))
    gs = np.ascontiguousarray(np.asarray(L0_ffn_gate_s, f32).reshape(IT, 128))
    us = np.ascontiguousarray(np.asarray(L0_ffn_up_s, f32).reshape(IT, 128))
    ds = np.ascontiguousarray(np.asarray(L0_ffn_down_s, f32).reshape(1, D))
    idf = np.eye(128, dtype=f32)
    ones2 = np.ones((1, DQ), dtype=_bf)

    in_maps = []
    for c in range(8):
        b, base = c // 4, (c % 4) * NQ
        xb = x[b]
        xqc = np.ascontiguousarray(xb[base:base + NQ])
        xTc = np.ascontiguousarray(np.roll(xb, -base, axis=0).T)
        tg = base + np.arange(NQ, dtype=np.int64)[:, None]
        sg = ((base + np.arange(T, dtype=np.int64)) % T)[None, :]
        maskc = np.where(sg > tg, f32(NEG), f32(0.0)).astype(_bf).reshape(NQT, 128, T)
        in_maps.append(dict(
            xq=xqc, xT=xTc, wq4=wq4, wkT=wkT, wvT=wvT, prj=prj, bqd=bq,
            bkd=bk, bvd=bv, gTd=gT, uTd=uT, dTd=dT, gsd=gs, usd=us, dsd=ds,
            msk=maskc, idfd=idf, onesd=ones2))

    nc = build_program()
    res = run_bass_kernel_spmd(nc, in_maps, core_ids=list(range(8)))
    outp = np.empty((2, T, D), dtype=f32)
    for c in range(8):
        b, base = c // 4, (c % 4) * NQ
        outp[b, base:base + NQ] = np.asarray(res.results[c]["outd"], dtype=f32)
    return outp


# version 09 (final)
# speedup vs baseline: 1.1854x; 1.0048x over previous
import math
import numpy as np
import ml_dtypes
import concourse.bass as bass
import concourse.mybir as mybir
from concourse.bass_utils import run_bass_kernel_spmd
from contextlib import ExitStack

F32 = mybir.dt.float32
BF16 = mybir.dt.bfloat16
AF = mybir.ActivationFunctionType
OP = mybir.AluOpType
AX = mybir.AxisListType

D = 2048
DT = 16
DQ = 512
T = 4096
NCH = 8
NQ = 1024
NQT = 8
II = 5632
IT = 44
EPS = 1e-5
RSQ = math.sqrt(128.0)
NEG = -1.0e30

ENGS = ("sync", "gpsimd", "scalar", "tensor", "vector")


class Op:
    __slots__ = ("eng", "fn", "waits", "incs")

    def __init__(self, eng, fn, waits, incs):
        self.eng, self.fn, self.waits, self.incs = eng, fn, list(waits), list(incs)


class Builder:
    def __init__(self, nc, es):
        self.nc = nc
        self.es = es
        self.sems = {}
        self.cnt = {}
        for s in ("pe", "ve", "se", "ge"):
            self.sem(s)
        self.cur = None

    def sem(self, name):
        if name not in self.sems:
            self.sems[name] = self.es.enter_context(self.nc.semaphore(name))
            self.cnt[name] = 0

    def begin(self):
        self.cur = []

    def op(self, eng, fn, waits=(), incs=()):
        own = {"vector": "ve", "scalar": "se", "gpsimd": "ge"}.get(eng)
        if own is not None and fn is not None and self.cnt[own] > 0:
            waits = list(waits) + [(own, self.cnt[own])]
        for (s, k) in incs:
            self.cnt[s] += k
        self.cur.append(Op(eng, fn, waits, incs))

    def dma(self, eng, fn, sem, waits=()):
        self.sem(sem)
        self.op(eng, fn, waits, [(sem, 16)])
        return self.cnt[sem]

    def m(self, sem):
        return self.cnt.get(sem, 0)

    def bar(self):
        return dict(self.cnt)

    def bar_ops(self, marks):
        w = [(s, v) for s, v in marks.items() if v > 0]
        for e in ENGS:
            self.op(e, None, w, ())

    def emit(self):
        ops = self.cur
        self.cur = None
        with self.nc.Block() as blk:
            for eng_name in ENGS:
                eops = [o for o in ops if o.eng == eng_name]
                if not eops:
                    continue

                def body(e, eops=eops):
                    lastw = {}
                    for o in eops:
                        for (s, v) in o.waits:
                            if v > 0 and lastw.get(s, 0) < v:
                                e.wait_ge(self.sems[s], v)
                                lastw[s] = v
                        if o.fn is not None:
                            ins = o.fn(e)
                            for (s, k) in o.incs:
                                ins = ins.then_inc(self.sems[s], k)
                getattr(blk, eng_name)(body)


def build_program(dbg=False):
    nc = bass.Bass("TRN2", target_bir_lowering=False)
    _scr = "ExternalOutput" if dbg else "Internal"

    a = lambda h: h.ap()
    xq = a(nc.dram_tensor("xq", [NQ, D], F32, kind="ExternalInput"))
    xT = a(nc.dram_tensor("xT", [D, T], F32, kind="ExternalInput"))
    wq4 = a(nc.dram_tensor("wq4", [4, D, DQ], F32, kind="ExternalInput"))
    wkT = a(nc.dram_tensor("wkT", [D, DQ], F32, kind="ExternalInput"))
    wvT = a(nc.dram_tensor("wvT", [D, DQ], F32, kind="ExternalInput"))
    prj = a(nc.dram_tensor("prj", [D, D], F32, kind="ExternalInput"))
    bqd = a(nc.dram_tensor("bqd", [16, 128], F32, kind="ExternalInput"))
    bkd = a(nc.dram_tensor("bkd", [4, 128], F32, kind="ExternalInput"))
    bvd = a(nc.dram_tensor("bvd", [1, DQ], F32, kind="ExternalInput"))
    gTd = a(nc.dram_tensor("gTd", [D, II], F32, kind="ExternalInput"))
    uTd = a(nc.dram_tensor("uTd", [D, II], F32, kind="ExternalInput"))
    dTd = a(nc.dram_tensor("dTd", [II, D], F32, kind="ExternalInput"))
    gsd = a(nc.dram_tensor("gsd", [IT, 128], F32, kind="ExternalInput"))
    usd = a(nc.dram_tensor("usd", [IT, 128], F32, kind="ExternalInput"))
    dsd = a(nc.dram_tensor("dsd", [1, D], F32, kind="ExternalInput"))
    msk = a(nc.dram_tensor("msk", [NQT, 128, T], BF16, kind="ExternalInput"))
    idfd = a(nc.dram_tensor("idfd", [128, 128], F32, kind="ExternalInput"))
    onesd = a(nc.dram_tensor("onesd", [1, DQ], BF16, kind="ExternalInput"))
    outd = a(nc.dram_tensor("outd", [NQ, D], F32, kind="ExternalOutput"))

    x1fd = a(nc.dram_tensor("x1fd", [NQ, D], F32, kind=_scr))
    x1bd = a(nc.dram_tensor("x1bd", [NQ, D], BF16, kind=_scr))
    ffnd = a(nc.dram_tensor("ffnd", [NQ, D], F32, kind=_scr))

    wq4_r = wq4.rearrange("r (dt p) j -> p r dt j", p=128)
    prj_t = prj.rearrange("(tt p) o -> tt p o", p=128)
    xT_r = xT.rearrange("(dt p) t -> p dt t", p=128)
    wk_r = wkT.rearrange("(dt p) o -> p dt o", p=128)
    wv_r = wvT.rearrange("(dt p) o -> p dt o", p=128)
    gT_r = gTd.rearrange("(dt p) i -> p dt i", p=128)
    uT_r = uTd.rearrange("(dt p) i -> p dt i", p=128)
    dT_r = dTd.rearrange("(it p) o -> p it o", p=128)
    xq_r = xq.rearrange("(q p) o -> q p o", p=128)
    x1f_r = x1fd.rearrange("(q p) o -> q p o", p=128)
    x1b_r = x1bd.rearrange("(q p) o -> q p o", p=128)
    ffn_r = ffnd.rearrange("(q p) o -> q p o", p=128)
    out_r = outd.rearrange("(q p) o -> q p o", p=128)

    es = ExitStack()
    with es:
        B = Builder(nc, es)
        sb = lambda n, sh, dt: es.enter_context(nc.sbuf_tensor(n, sh, dt))
        gs_col = sb("gs_col", [128, IT], F32)
        us_col = sb("us_col", [128, IT], F32)
        onesf = sb("onesf", [1, 128], F32)
        sum8 = sb("sum8", [128, NQT], F32)
        sq8 = sb("sq8", [128, NQT], F32)
        nmax8 = sb("nmax8", [128, NQT], F32)
        rec8 = sb("rec8", [128, NQT], F32)
        nmean8 = sb("nmean8", [128, NQT], F32)
        var8 = sb("var8", [128, NQT], F32)
        rstd8 = sb("rstd8", [128, NQT], F32)
        tmp8 = sb("tmp8", [128, NQT], F32)
        l1s = sb("l1s", [128, NQT], F32)
        l1q = sb("l1q", [128, NQT], F32)
        l1nm = sb("l1nm", [128, NQT], F32)
        l1v = sb("l1v", [128, NQT], F32)
        l1t = sb("l1t", [128, NQT], F32)
        l1r = sb("l1r", [128, NQT], F32)

        esA = ExitStack()
        with esA:
            sA = lambda n, sh, dt: esA.enter_context(nc.sbuf_tensor(n, sh, dt))
            kT = sA("kT", [128, NCH, 4, 512], BF16)
            vv = sA("vv", [128, 32, DQ], BF16)
            qsT = sA("qsT", [128, 2, 4, 512], BF16)
            projsum = sA("projsum", [128, 4, D], BF16)
            idf_sb = sA("idf_sb", [128, 128], F32)
            idb_sb = sA("idb_sb", [128, 128], BF16)
            bqs_col = sA("bqs_col", [128, 4], F32)
            bk_col = sA("bk_col", [128, 4], F32)
            bv_bc = sA("bv_bc", [128, DQ], F32)

            esW = ExitStack()
            with esW:
                sW = lambda n, sh, dt: esW.enter_context(nc.sbuf_tensor(n, sh, dt))
                wqsT = sW("wqsT", [128, DT, DQ], BF16)
                wk_sb = sW("wk_sb", [128, DT, DQ], BF16)
                wv_sb = sW("wv_sb", [128, DT, DQ], BF16)
                wq4buf = [sW("wq4b0", [128, 4, DQ], BF16),
                          sW("wq4b1", [128, 4, DQ], BF16)]

                esA0 = ExitStack()
                with esA0:
                    s0 = lambda n, sh, dt: esA0.enter_context(nc.sbuf_tensor(n, sh, dt))
                    gsld = s0("gsld", [IT, 128], F32)
                    usld = s0("usld", [IT, 128], F32)
                    bqld = s0("bqld", [16, 128], F32)
                    bkld = s0("bkld", [4, 128], F32)
                    bvrow = s0("bvrow", [1, DQ], F32)
                    bqtc = s0("bqtc", [128, 16], F32)
                    ppB = esA0.enter_context(nc.psum_tensor("ppB", [128, 512], F32))

                    B.begin()
                    for (dst, src) in ((idf_sb, idfd), (gsld, gsd), (usld, usd),
                                       (bqld, bqd), (bkld, bkd), (bvrow, bvd)):
                        B.dma("sync",
                              (lambda dst, src: lambda e: e.dma_start(out=dst[:], in_=src))(dst, src),
                              "sa0")
                    mA0 = B.m("sa0")

                    B.op("vector", lambda e: e.tensor_copy(idb_sb[:], idf_sb[:]),
                         [("sa0", mA0)], [("ve", 1)])
                    B.op("vector", lambda e: e.memset(onesf[:], 1.0), (), [("ve", 1)])
                    mv_ones = B.m("ve")
                    B.op("tensor", lambda e: e.transpose(ppB[:, 0:IT], gsld[:],
                                                         idf_sb[0:IT, 0:IT]),
                         [("sa0", mA0)], [("pe", 1)])
                    mp1 = B.m("pe")
                    B.op("vector", lambda e: e.tensor_copy(gs_col[:], ppB[:, 0:IT]),
                         [("pe", mp1)], [("ve", 1)])
                    mv1 = B.m("ve")
                    B.op("tensor", lambda e: e.transpose(ppB[:, 0:IT], usld[:],
                                                         idf_sb[0:IT, 0:IT]),
                         [("ve", mv1)], [("pe", 1)])
                    mp2 = B.m("pe")
                    B.op("vector", lambda e: e.tensor_copy(us_col[:], ppB[:, 0:IT]),
                         [("pe", mp2)], [("ve", 1)])
                    mv2 = B.m("ve")
                    B.op("tensor", lambda e: e.transpose(ppB[:, 0:16], bqld[:],
                                                         idf_sb[0:16, 0:16]),
                         [("ve", mv2)], [("pe", 1)])
                    mp3 = B.m("pe")
                    B.op("vector", lambda e: e.tensor_copy(bqtc[:], ppB[:, 0:16]),
                         [("pe", mp3)], [("ve", 1)])
                    mv3 = B.m("ve")
                    B.op("tensor", lambda e: e.transpose(ppB[:, 0:4], bkld[:],
                                                         idf_sb[0:4, 0:4]),
                         [("ve", mv3)], [("pe", 1)])
                    mp4 = B.m("pe")
                    B.op("vector", lambda e: e.tensor_copy(bk_col[:], ppB[:, 0:4]),
                         [("pe", mp4)], [("ve", 1)])
                    B.op("vector", lambda e: e.tensor_add(
                        bqs_col[:], bqtc[:, 0:4], bqtc[:, 4:8]), (), [("ve", 1)])
                    B.op("vector", lambda e: e.tensor_add(
                        bqs_col[:], bqs_col[:], bqtc[:, 8:12]), (), [("ve", 1)])
                    B.op("vector", lambda e: e.tensor_add(
                        bqs_col[:], bqs_col[:], bqtc[:, 12:16]), (), [("ve", 1)])
                    mv4 = B.m("ve")
                    B.op("tensor", lambda e: e.matmul(ppB[:], onesf[:], bvrow[:],
                                                      start=True, stop=True),
                         [("ve", max(mv_ones, mv4)), ("sa0", mA0)], [("pe", 1)])
                    mp5 = B.m("pe")
                    B.op("vector", lambda e: e.tensor_copy(bv_bc[:], ppB[:]),
                         [("pe", mp5)], [("ve", 1)])
                    mv_a0 = B.m("ve")
                    mp_a0 = B.m("pe")
                    B.emit()

                esA1 = ExitStack()
                with esA1:
                    s1 = lambda n, sh, dt: esA1.enter_context(nc.sbuf_tensor(n, sh, dt))
                    xch = [s1("xch0", [128, DT, 512], BF16),
                           s1("xch1", [128, DT, 512], BF16)]
                    kps = esA1.enter_context(nc.psum_tensor("kps", [128, 4, 512], F32))
                    vps = esA1.enter_context(nc.psum_tensor("vps", [128, 4, 512], F32))

                    B.begin()
                    proc = [2, 3, 4, 5, 6, 7, 0, 1]
                    xuse = [0, 0]
                    mk_kfree = 0
                    mk_vfree = 0
                    wqf_free = [0, 0]
                    fold_dt = [0]
                    wq_done = [0]

                    mx_pre = {}
                    mx_pre[0] = B.dma("gpsimd",
                                      lambda e: e.dma_start(out=xch[0][:],
                                                            in_=xT_r[:, :, 2*512:3*512]),
                                      "sxc0", [("pe", mp_a0), ("ve", mv_a0)])
                    m_wk = B.dma("gpsimd", lambda e: e.dma_start(out=wk_sb[:], in_=wk_r),
                                 "swk")
                    mx_pre[1] = B.dma("gpsimd",
                                      lambda e: e.dma_start(out=xch[1][:],
                                                            in_=xT_r[:, :, 3*512:4*512]),
                                      "sxc1")
                    m_wv = B.dma("gpsimd", lambda e: e.dma_start(out=wv_sb[:], in_=wv_r),
                                 "swv")

                    def plan_wq_fold():
                        dt_ = fold_dt[0]
                        fold_dt[0] += 1
                        fbi = dt_ % 2
                        swq = "swq%d" % fbi
                        mwq = B.dma("gpsimd",
                                    (lambda dt_, fbi: lambda e: e.dma_start(
                                        out=wq4buf[fbi][:], in_=wq4_r[:, :, dt_, :]))(dt_, fbi),
                                    swq, [("pe", wqf_free[fbi])])
                        nonlocal_mk = mk_kfree_ref[0]
                        for r in range(4):
                            B.op("tensor",
                                 (lambda fbi, r: lambda e: e.matmul(
                                     kps[:, 0, :], idb_sb[:], wq4buf[fbi][:, r, :],
                                     start=(r == 0), stop=(r == 3)))(fbi, r),
                                 [(swq, mwq), ("ve", mv_a0), ("se", nonlocal_mk)]
                                 if r == 0 else (),
                                 [("pe", 1)] if r == 3 else ())
                        wqf_free[fbi] = B.m("pe")
                        B.op("scalar",
                             (lambda dt_: lambda e: e.activation(
                                 out=wqsT[:, dt_, :], in_=kps[:, 0, :],
                                 func=AF.Copy))(dt_),
                             [("pe", wqf_free[fbi])], [("se", 1)])
                        mk_kfree_ref[0] = B.m("se")
                        if fold_dt[0] == DT:
                            wq_done[0] = B.m("se")

                    mk_kfree_ref = [0]
                    for pi, rc in enumerate(proc):
                        bi = pi % 2
                        sxc = "sxc%d" % bi
                        if pi in mx_pre:
                            mx = mx_pre[pi]
                        else:
                            mx = B.dma("gpsimd",
                                       (lambda rc, bi: lambda e: e.dma_start(
                                           out=xch[bi][:],
                                           in_=xT_r[:, :, rc*512:(rc+1)*512]))(rc, bi),
                                       sxc,
                                       [("pe", max(xuse[bi], mp_a0)), ("ve", mv_a0)])
                        xc = xch[bi]
                        mk_kfree = mk_kfree_ref[0]
                        for dq in range(4):
                            w0 = [(sxc, mx), ("swk", m_wk), ("se", mk_kfree),
                                  ("ve", mv_a0), ("sa0", mA0)]
                            for ct in range(DT):
                                B.op("tensor",
                                     (lambda dq, ct, xc: lambda e: e.matmul(
                                         kps[:, dq, :], wk_sb[:, ct, dq*128:(dq+1)*128],
                                         xc[:, ct, :], start=(ct == 0),
                                         stop=(ct == DT-1)))(dq, ct, xc),
                                     w0 if (ct == 0 and dq == 0) else (),
                                     [("pe", 1)] if (ct == DT-1 and dq == 3) else ())
                        mp_k = B.m("pe")
                        for dq in range(4):
                            B.op("scalar",
                                 (lambda rc, dq: lambda e: e.activation(
                                     out=kT[:, rc, dq], in_=kps[:, dq, :],
                                     func=AF.Identity, bias=bk_col[:, dq:dq+1]))(rc, dq),
                                 [("pe", mp_k)], [("se", 1)])
                        mk_kfree_ref[0] = B.m("se")
                        for sv in range(4):
                            w0 = [(sxc, mx), ("swv", m_wv), ("ve", max(mk_vfree, mv_a0))]
                            for ct in range(DT):
                                B.op("tensor",
                                     (lambda sv, ct, xc: lambda e: e.matmul(
                                         vps[:, sv, :], xc[:, ct, sv*128:(sv+1)*128],
                                         wv_sb[:, ct, :], start=(ct == 0),
                                         stop=(ct == DT-1)))(sv, ct, xc),
                                     w0 if (ct == 0 and sv == 0) else (),
                                     [("pe", 1)] if (ct == DT-1 and sv == 3) else ())
                        mp_v = B.m("pe")
                        for sv in range(4):
                            B.op("vector",
                                 (lambda rc, sv: lambda e: e.scalar_tensor_tensor(
                                     out=vv[:, rc*4+sv, :], in0=vps[:, sv, :],
                                     scalar=1.0, in1=bv_bc[:],
                                     op0=OP.mult, op1=OP.add))(rc, sv),
                                 [("pe", mp_v)], [("ve", 1)])
                        mk_vfree = B.m("ve")
                        if pi < 4:
                            for _ in range(4):
                                plan_wq_fold()
                        if rc in (0, 1):
                            for dq in range(4):
                                w0 = [("se", max(mk_kfree_ref[0], wq_done[0]))]
                                for ct in range(DT):
                                    B.op("tensor",
                                         (lambda dq, ct, xc: lambda e: e.matmul(
                                             kps[:, dq, :], wqsT[:, ct, dq*128:(dq+1)*128],
                                             xc[:, ct, :], start=(ct == 0),
                                             stop=(ct == DT-1)))(dq, ct, xc),
                                         w0 if (ct == 0 and dq == 0) else (),
                                         [("pe", 1)] if (ct == DT-1 and dq == 3) else ())
                            mp_q = B.m("pe")
                            for dq in range(4):
                                B.op("scalar",
                                     (lambda rc, dq: lambda e: e.activation(
                                         out=qsT[:, rc, dq], in_=kps[:, dq, :],
                                         func=AF.Identity, bias=bqs_col[:, dq:dq+1]))(rc, dq),
                                     [("pe", mp_q)], [("se", 1)])
                            mk_kfree_ref[0] = B.m("se")
                        xuse[bi] = B.m("pe")
                    bar1 = B.bar()
                    B.emit()

            esB = ExitStack()
            with esB:
                sB = lambda n, sh, dt: esB.enter_context(nc.sbuf_tensor(n, sh, dt))
                masks = sB("masks", [128, 2, T], BF16)
                scores = sB("scores", [128, T], F32)
                Pq = sB("Pq", [128, 2, T], BF16)
                aT_sb = sB("aT_sb", [128, 2, 1024], BF16)
                zT_sb = sB("zT_sb", [128, 4, 256], BF16)
                zn_sb = sB("zn_sb", [128, 2, 512], BF16)
                xqp = sB("xqp", [128, 2, 512], F32)
                y1s = sB("y1s", [128, 2, D], F32)
                y1s16 = sB("y1s16", [128, 2, D], BF16)
                sqscr = sB("sqscr", [128, D], F32)
                scps = esB.enter_context(nc.psum_tensor("scps", [128, 2, 512], F32))
                atps = [esB.enter_context(nc.psum_tensor("atpsA", [128, 1024], BF16)),
                        esB.enter_context(nc.psum_tensor("atpsB", [128, 1024], BF16))]
                zpn = esB.enter_context(nc.psum_tensor("zpn", [128, 2, 512], F32))
                pjps = esB.enter_context(nc.psum_tensor("pjps", [128, 2, 512], F32))

                B.begin()
                B.bar_ops(bar1)
                bankmap = [scps[:, 0, :], scps[:, 1, :],
                           pjps[:, 0, :], pjps[:, 1, :]]
                pjf_free = [0, 0]
                fold_ev = [0, 0, 0, 0]
                for g_ in range(4):
                    for r_ in range(4):
                        bi_ = r_ % 2
                        spj = "spj%d" % bi_
                        mpj = B.dma("gpsimd",
                                    (lambda g_, r_, bi_: lambda e: e.dma_start(
                                        out=Pq[:, bi_, 0:D], in_=prj_t[r_ * 4 + g_]))(g_, r_, bi_),
                                    spj, [("pe", pjf_free[bi_])])
                        for c_ in range(4):
                            B.op("tensor",
                                 (lambda bi_, r_, c_: lambda e: e.matmul(
                                     bankmap[c_], idb_sb[:],
                                     Pq[:, bi_, c_*512:(c_+1)*512],
                                     start=(r_ == 0), stop=(r_ == 3)))(bi_, r_, c_),
                                 [(spj, mpj), ("ve", max(fold_ev))] if c_ == 0 else (),
                                 [("pe", 1)] if c_ == 3 else ())
                        pjf_free[bi_] = B.m("pe")
                    mp_f = B.m("pe")
                    for c_ in range(4):
                        B.op("vector",
                             (lambda g_, c_: lambda e: e.tensor_copy(
                                 projsum[:, g_, c_*512:(c_+1)*512], bankmap[c_]))(g_, c_),
                             [("pe", mp_f)], [("ve", 1)])
                        fold_ev[c_] = B.m("ve")
                mv_fold = B.m("ve")
                mp_fold = B.m("pe")

                mask_free = [0, 0]
                sc_free = [mv_fold, mv_fold]
                exp_prev = 0
                p_free = [mp_fold, mp_fold]
                atp_free = [0, 0]
                at_ready = [0, 0]
                pj_free = [mv_fold, mv_fold]
                xq_free = [0, 0]
                zevac = 0
                slot_st = [0, 0]
                zT_used = 0

                def plan_scores(g):
                    nonlocal exp_prev
                    for q in (2 * g, 2 * g + 1):
                        sl = q % 2
                        smk = "smk%d" % sl
                        mmsk = B.dma("sync",
                                     (lambda q, sl: lambda e: e.dma_start(
                                         out=masks[:, sl, :], in_=msk[q]))(q, sl),
                                     smk, [("ve", mask_free[sl])])
                        for ch in range(NCH):
                            cs = ch % 2
                            for dqt in range(4):
                                B.op("tensor",
                                     (lambda q, ch, cs, dqt: lambda e: e.matmul(
                                         scps[:, cs, :],
                                         qsT[:, q//4, dqt, (q%4)*128:(q%4+1)*128],
                                         kT[:, ch, dqt, :],
                                         start=(dqt == 0), stop=(dqt == 3)))(q, ch, cs, dqt),
                                     [("ve", sc_free[cs])] if dqt == 0 else (),
                                     [("pe", 1)] if dqt == 3 else ())
                            mp_sc = B.m("pe")
                            B.op("vector",
                                 (lambda ch, cs, sl: lambda e: e.scalar_tensor_tensor(
                                     out=scores[:, ch*512:(ch+1)*512], in0=scps[:, cs, :],
                                     scalar=1.0, in1=masks[:, sl, ch*512:(ch+1)*512],
                                     op0=OP.mult, op1=OP.add))(ch, cs, sl),
                                 [("pe", mp_sc), (smk, mmsk), ("se", exp_prev)],
                                 [("ve", 1)])
                            sc_free[cs] = B.m("ve")
                        mask_free[sl] = B.m("ve")
                        B.op("vector",
                             (lambda q: lambda e: e.tensor_reduce(
                                 nmax8[:, q:q+1], scores[:], axis=AX.X, op=OP.max,
                                 negate=True))(q),
                             (), [("ve", 1)])
                        mv_max = B.m("ve")
                        B.op("scalar",
                             (lambda q, sl: lambda e: e.activation(
                                 out=Pq[:, sl, :], in_=scores[:], func=AF.Exp,
                                 bias=nmax8[:, q:q+1], scale=1.0,
                                 accum_out=sum8[:, q:q+1]))(q, sl),
                             [("ve", mv_max), ("pe", p_free[sl])], [("se", 1)])
                        exp_prev = B.m("se")
                        B.op("vector",
                             (lambda q: lambda e: e.tensor_scalar(
                                 out=tmp8[:, q:q+1], in0=sum8[:, q:q+1], scalar1=RSQ,
                                 scalar2=None, op0=OP.mult))(q),
                             [("se", exp_prev)], [("ve", 1)])
                        B.op("vector",
                             (lambda q: lambda e: e.reciprocal(
                                 rec8[:, q:q+1], tmp8[:, q:q+1]))(q),
                             (), [("ve", 1)])

                def plan_z(g):
                    nonlocal zevac

                    def plan_tr(blk):
                        bk = blk % 2
                        atp = atps[bk]
                        for j in range(4):
                            st = blk * 4 + j
                            for qi in range(2):
                                B.op("tensor",
                                     (lambda st, j, qi, atp: lambda e: e.transpose(
                                         atp[:, (j*2+qi)*128:(j*2+qi+1)*128],
                                         Pq[:, qi, st*128:(st+1)*128], idb_sb[:]))(st, j, qi, atp),
                                     [("ve", atp_free[bk]), ("se", exp_prev)]
                                     if (j == 0 and qi == 0) else (),
                                     [("pe", 1)] if (j == 3 and qi == 1) else ())
                        mp_tr = B.m("pe")
                        B.op("vector",
                             (lambda bk, atp: lambda e: e.tensor_copy(
                                 aT_sb[:, bk, :], atp[:]))(bk, atp),
                             [("pe", mp_tr)], [("ve", 1)])
                        at_ready[bk] = B.m("ve")
                        atp_free[bk] = at_ready[bk]

                    def plan_mm(blk):
                        bk = blk % 2
                        for j in range(4):
                            st = blk * 4 + j
                            for tq in range(2):
                                B.op("tensor",
                                     (lambda st, bk, j, tq: lambda e: e.matmul(
                                         zpn[:, tq, :],
                                         aT_sb[:, bk, (j*2+tq)*128:(j*2+tq+1)*128],
                                         vv[:, st, :], start=(st == 0),
                                         stop=(st == 31)))(st, bk, j, tq),
                                     [("ve", max(at_ready[bk], zevac))]
                                     if (j == 0 and tq == 0) else (),
                                     [("pe", 1)] if (st == 31 and tq == 1) else ())

                    plan_tr(0)
                    for blk in range(1, 8):
                        plan_tr(blk)
                        plan_mm(blk - 1)
                    plan_mm(7)
                    p_free[0] = p_free[1] = B.m("pe")
                    mp_z = B.m("pe")
                    B.op("vector", lambda e: e.tensor_copy(zn_sb[:], zpn[:]),
                         [("pe", mp_z)], [("ve", 1)])
                    mv_zn = B.m("ve")
                    for tq in range(2):
                        for dv in range(4):
                            B.op("tensor",
                                 (lambda tq, dv: lambda e: e.transpose(
                                     atps[tq][:, dv*128:(dv+1)*128],
                                     zn_sb[:, tq, dv*128:(dv+1)*128], idb_sb[:]))(tq, dv),
                                 [("ve", mv_zn), ("pe", zT_used)] if dv == 0 else (),
                                 [("pe", 1)] if dv == 3 else ())
                        mp_zt = B.m("pe")
                        B.op("vector",
                             (lambda tq: lambda e: e.tensor_copy(
                                 zT_sb[:, :, tq*128:(tq+1)*128],
                                 atps[tq][:, 0:512].rearrange("p (dv t) -> p dv t", dv=4)))(tq),
                             [("pe", mp_zt)], [("ve", 1)])
                    zevac = B.m("ve")
                    atp_free[0] = atp_free[1] = zevac

                def plan_proj_ln1(g):
                    nonlocal zT_used
                    for tq in range(2):
                        q = 2 * g + tq
                        sx = "sx%d" % tq
                        for dc in range(4):
                            ps = dc % 2
                            sxq = "sxq%d" % ps
                            mxq = B.dma("sync",
                                        (lambda q, dc, ps: lambda e: e.dma_start(
                                            out=xqp[:, ps, :],
                                            in_=xq_r[q][:, dc*512:(dc+1)*512]))(q, dc, ps),
                                        sxq, [("ve", xq_free[ps])])
                            for dv in range(4):
                                B.op("tensor",
                                     (lambda tq, dc, ps, dv: lambda e: e.matmul(
                                         pjps[:, ps, :], zT_sb[:, dv, tq*128:(tq+1)*128],
                                         projsum[:, dv, dc*512:(dc+1)*512],
                                         start=(dv == 0), stop=(dv == 3)))(tq, dc, ps, dv),
                                     [("ve", max(zevac, pj_free[ps]))] if dv == 0 else (),
                                     [("pe", 1)] if dv == 3 else ())
                            mp_pj = B.m("pe")
                            B.op("vector",
                                 (lambda q, tq, dc, ps: lambda e: e.scalar_tensor_tensor(
                                     out=y1s[:, tq, dc*512:(dc+1)*512], in0=pjps[:, ps, :],
                                     scalar=rec8[:, q:q+1], in1=xqp[:, ps, :],
                                     op0=OP.mult, op1=OP.add))(q, tq, dc, ps),
                                 [("pe", mp_pj), (sxq, mxq), (sx, slot_st[tq])],
                                 [("ve", 1)])
                            pj_free[ps] = B.m("ve")
                            xq_free[ps] = B.m("ve")
                        mv_y1 = B.m("ve")
                    zT_used = B.m("pe")

                def plan_ln1(g):
                    for tq in range(2):
                        q = 2 * g + tq
                        sx = "sx%d" % tq
                        mv_ydone = B.m("ve")
                        B.op("vector",
                             (lambda q, tq: lambda e: e.tensor_reduce(
                                 l1s[:, q:q+1], y1s[:, tq, :], axis=AX.X, op=OP.add))(q, tq),
                             (), [("ve", 1)])
                        B.op("scalar",
                             (lambda q, tq: lambda e: e.activation(
                                 out=sqscr[:], in_=y1s[:, tq, :], func=AF.Square,
                                 accum_out=l1q[:, q:q+1]))(q, tq),
                             [("ve", mv_ydone)], [("se", 1)])
                        ms_sq = B.m("se")
                        B.op("gpsimd",
                             (lambda q: lambda e: e.tensor_scalar(
                                 out=l1nm[:, q:q+1], in0=l1s[:, q:q+1],
                                 scalar1=-1.0/D, scalar2=None, op0=OP.mult))(q),
                             [("ve", B.m("ve"))], [("ge", 1)])
                        B.op("gpsimd",
                             (lambda q: lambda e: e.tensor_mul(
                                 l1t[:, q:q+1], l1nm[:, q:q+1], l1nm[:, q:q+1]))(q),
                             (), [("ge", 1)])
                        B.op("gpsimd",
                             (lambda q: lambda e: e.tensor_scalar(
                                 out=l1v[:, q:q+1], in0=l1q[:, q:q+1],
                                 scalar1=1.0/D, scalar2=None, op0=OP.mult))(q),
                             [("se", ms_sq)], [("ge", 1)])
                        B.op("gpsimd",
                             (lambda q: lambda e: e.tensor_sub(
                                 l1v[:, q:q+1], l1v[:, q:q+1], l1t[:, q:q+1]))(q),
                             (), [("ge", 1)])
                        B.op("gpsimd",
                             (lambda q: lambda e: e.tensor_scalar(
                                 out=l1v[:, q:q+1], in0=l1v[:, q:q+1],
                                 scalar1=float(EPS), scalar2=None, op0=OP.add))(q),
                             (), [("ge", 1)])
                        mg_var = B.m("ge")
                        B.op("scalar",
                             (lambda q: lambda e: e.activation(
                                 out=l1t[:, q:q+1], in_=l1v[:, q:q+1], func=AF.Ln))(q),
                             [("ge", mg_var)], [("se", 1)])
                        B.op("scalar",
                             (lambda q: lambda e: e.activation(
                                 out=l1r[:, q:q+1], in_=l1t[:, q:q+1], func=AF.Exp,
                                 scale=-0.5))(q),
                             (), [("se", 1)])
                        ms_rs = B.m("se")
                        B.op("gpsimd",
                             (lambda q, tq: lambda e: e.tensor_scalar(
                                 out=y1s[:, tq, :], in0=y1s[:, tq, :],
                                 scalar1=l1nm[:, q:q+1], scalar2=l1r[:, q:q+1],
                                 op0=OP.add, op1=OP.mult))(q, tq),
                             [("se", ms_rs)], [("ge", 1)])
                        B.op("gpsimd",
                             (lambda tq: lambda e: e.tensor_copy(
                                 y1s16[:, tq, :], y1s[:, tq, :]))(tq),
                             (), [("ge", 1)])
                        mg_a = B.m("ge")
                        B.dma("sync",
                              (lambda q, tq: lambda e: e.dma_start(
                                  out=x1f_r[q], in_=y1s[:, tq, :]))(q, tq),
                              sx, [("ge", mg_a)])
                        B.dma("sync",
                              (lambda q, tq: lambda e: e.dma_start(
                                  out=x1b_r[q], in_=y1s16[:, tq, :]))(q, tq),
                              sx)
                        slot_st[tq] = B.m(sx)

                pending = None
                for g in range(4):
                    plan_scores(g)
                    if pending is not None:
                        plan_proj_ln1(pending)
                    plan_z(g)
                    if pending is not None:
                        plan_ln1(pending)
                    pending = g
                plan_proj_ln1(pending)
                plan_ln1(pending)
                bar2 = B.bar()
                B.emit()

        esD = ExitStack()
        with esD:
            sD = lambda n, sh, dt: esD.enter_context(nc.sbuf_tensor(n, sh, dt))
            hT = sD("hT", [128, IT, NQ], BF16)

            esD1 = ExitStack()
            with esD1:
                s1 = lambda n, sh, dt: esD1.enter_context(nc.sbuf_tensor(n, sh, dt))
                x1T = s1("x1T", [128, DT, NQ], BF16)
                gstr = [s1("gstr0", [128, DT, 128], BF16),
                        s1("gstr1", [128, DT, 128], BF16)]
                ustr = [s1("ustr0", [128, DT, 128], BF16),
                        s1("ustr1", [128, DT, 128], BF16)]
                silu_sb = [s1("silu0", [128, 512], F32),
                           s1("silu1", [128, 512], F32)]
                gups = esD1.enter_context(nc.psum_tensor("gups", [128, 4, 512], F32))

                B.begin()
                B.bar_ops(bar2)
                for dt_ in range(DT):
                    B.dma("sync",
                          (lambda dt_: lambda e: e.dma_start(
                              out=x1T[:, dt_, :], in_=x1bd[:, dt_*128:(dt_+1)*128],
                              transpose=True))(dt_),
                          "sx1T")
                mx1T = B.m("sx1T")
                str_free = [0, 0]
                bank_free = [0, 0, 0, 0]
                silu_free = [0, 0]
                for it in range(IT):
                    bi = it % 2
                    sgs = "sgs%d" % bi
                    B.dma("gpsimd",
                          (lambda it, bi: lambda e: e.dma_start(
                              out=gstr[bi][:], in_=gT_r[:, :, it*128:(it+1)*128]))(it, bi),
                          sgs, [("pe", str_free[bi])])
                    mstr = B.dma("gpsimd",
                                 (lambda it, bi: lambda e: e.dma_start(
                                     out=ustr[bi][:], in_=uT_r[:, :, it*128:(it+1)*128]))(it, bi),
                                 sgs)
                    for tch in range(2):
                        gsl, usl = tch * 2, tch * 2 + 1
                        for ct in range(DT):
                            B.op("tensor",
                                 (lambda bi, tch, gsl, ct: lambda e: e.matmul(
                                     gups[:, gsl, :], gstr[bi][:, ct, :],
                                     x1T[:, ct, tch*512:(tch+1)*512],
                                     start=(ct == 0), stop=(ct == DT-1)))(bi, tch, gsl, ct),
                                 [(sgs, mstr), ("sx1T", mx1T), ("ve", bank_free[gsl])]
                                 if ct == 0 else (),
                                 [("pe", 1)] if ct == DT-1 else ())
                        mp_g = B.m("pe")
                        for ct in range(DT):
                            B.op("tensor",
                                 (lambda bi, tch, usl, ct: lambda e: e.matmul(
                                     gups[:, usl, :], ustr[bi][:, ct, :],
                                     x1T[:, ct, tch*512:(tch+1)*512],
                                     start=(ct == 0), stop=(ct == DT-1)))(bi, tch, usl, ct),
                                 [("ve", bank_free[usl])] if ct == 0 else (),
                                 [("pe", 1)] if ct == DT-1 else ())
                        mp_u = B.m("pe")
                        ssl = tch
                        B.op("scalar",
                             (lambda it, gsl, ssl: lambda e: e.activation(
                                 out=silu_sb[ssl][:], in_=gups[:, gsl, :], func=AF.Silu,
                                 scale=gs_col[:, it:it+1]))(it, gsl, ssl),
                             [("pe", mp_g), ("ve", silu_free[ssl])], [("se", 1)])
                        ms_sl = B.m("se")
                        B.op("vector",
                             (lambda it, tch, usl, ssl: lambda e: e.scalar_tensor_tensor(
                                 out=hT[:, it, tch*512:(tch+1)*512], in0=gups[:, usl, :],
                                 scalar=us_col[:, it:it+1], in1=silu_sb[ssl][:],
                                 op0=OP.mult, op1=OP.mult))(it, tch, usl, ssl),
                             [("pe", mp_u), ("se", ms_sl)], [("ve", 1)])
                        mvh = B.m("ve")
                        silu_free[ssl] = mvh
                        bank_free[usl] = mvh
                        bank_free[gsl] = mvh
                    str_free[bi] = B.m("pe")
                bar4 = B.bar()
                B.emit()

            esE = ExitStack()
            with esE:
                sE = lambda n, sh, dt: esE.enter_context(nc.sbuf_tensor(n, sh, dt))
                dstr = [sE("dstr0", [128, IT, 256], BF16),
                        sE("dstr1", [128, IT, 256], BF16),
                        sE("dstr2", [128, IT, 256], BF16)]
                ffns = sE("ffns", [128, 2, 256], F32)
                ds_b = sE("ds_b", [128, D], F32)
                dsrow = sE("dsrow", [1, D], F32)
                dnps = [esE.enter_context(nc.psum_tensor("dnps0", [128, 512], F32)),
                        esE.enter_context(nc.psum_tensor("dnps1", [128, 512], F32))]
                dspp = esE.enter_context(nc.psum_tensor("dspp", [128, 512], F32))

                B.begin()
                B.bar_ops(bar4)
                m_ds = B.dma("sync", lambda e: e.dma_start(out=dsrow[:], in_=dsd), "sds")
                pp_free = 0
                for k in range(4):
                    B.op("tensor",
                         (lambda k: lambda e: e.matmul(
                             dspp[:], onesf[:], dsrow[0:1, k*512:(k+1)*512],
                             start=True, stop=True))(k),
                         [("sds", m_ds), ("ve", pp_free)], [("pe", 1)])
                    mpk = B.m("pe")
                    B.op("vector",
                         (lambda k: lambda e: e.tensor_copy(
                             ds_b[:, k*512:(k+1)*512], dspp[:]))(k),
                         [("pe", mpk)], [("ve", 1)])
                    pp_free = B.m("ve")
                mv_dsb = B.m("ve")
                dstr_free = [0, 0, 0]
                dn_free = [0, 0]
                fst = [0, 0]
                for dc8 in range(8):
                    bi = dc8 % 3
                    sd = "sd%d" % bi
                    md = B.dma("gpsimd",
                               (lambda dc8, bi: lambda e: e.dma_start(
                                   out=dstr[bi][:], in_=dT_r[:, :, dc8*256:(dc8+1)*256]))(dc8, bi),
                               sd, [("pe", dstr_free[bi])])
                    for ts in range(8):
                        ps = (dc8 * 8 + ts) % 2
                        sff = "sff%d" % ps
                        for it in range(IT):
                            B.op("tensor",
                                 (lambda bi, ts, ps, it: lambda e: e.matmul(
                                     dnps[ps][:, 0:256], hT[:, it, ts*128:(ts+1)*128],
                                     dstr[bi][:, it, :],
                                     start=(it == 0), stop=(it == IT-1)))(bi, ts, ps, it),
                                 [(sd, md), ("ve", dn_free[ps])] if it == 0 else (),
                                 [("pe", 1)] if it == IT-1 else ())
                        mp_d = B.m("pe")
                        B.op("vector",
                             (lambda dc8, ps: lambda e: e.scalar_tensor_tensor(
                                 out=ffns[:, ps, :], in0=dnps[ps][:, 0:256], scalar=1.0,
                                 in1=ds_b[:, dc8*256:(dc8+1)*256],
                                 op0=OP.mult, op1=OP.mult))(dc8, ps),
                             [("pe", mp_d), (sff, fst[ps]), ("ve", mv_dsb)],
                             [("ve", 1)])
                        mv_f = B.m("ve")
                        dn_free[ps] = mv_f
                        B.dma("sync",
                              (lambda dc8, ts, ps: lambda e: e.dma_start(
                                  out=ffn_r[ts][:, dc8*256:(dc8+1)*256],
                                  in_=ffns[:, ps, :]))(dc8, ts, ps),
                              sff, [("ve", mv_f)])
                        fst[ps] = B.m(sff)
                    dstr_free[bi] = B.m("pe")
                bar5 = B.bar()
                B.emit()

        esF = ExitStack()
        with esF:
            sF = lambda n, sh, dt: esF.enter_context(nc.sbuf_tensor(n, sh, dt))
            y2t = sF("y2t", [128, NQT, D], F32)
            rx = sF("rx", [128, 2, D], F32)
            rf = sF("rf", [128, 2, D], F32)
            outs = sF("outs", [128, 2, D], F32)
            sqscr2 = sF("sqscr2", [128, D], F32)

            B.begin()
            B.bar_ops(bar5)
            r_free = [0, 0]
            for q in range(NQT):
                sl = q % 2
                sfr = "sfr%d" % sl
                B.dma("sync",
                      (lambda q, sl: lambda e: e.dma_start(out=rx[:, sl, :], in_=x1f_r[q]))(q, sl),
                      sfr, [("ve", r_free[sl])])
                m2 = B.dma("sync",
                           (lambda q, sl: lambda e: e.dma_start(out=rf[:, sl, :], in_=ffn_r[q]))(q, sl),
                           sfr)
                B.op("vector",
                     (lambda q, sl: lambda e: e.tensor_add(
                         y2t[:, q, :], rx[:, sl, :], rf[:, sl, :]))(q, sl),
                     [(sfr, m2)], [("ve", 1)])
                r_free[sl] = B.m("ve")
                B.op("vector",
                     (lambda q: lambda e: e.tensor_reduce(
                         sum8[:, q:q+1], y2t[:, q, :], axis=AX.X, op=OP.add))(q),
                     (), [("ve", 1)])
                mv_add = B.m("ve")
                B.op("scalar",
                     (lambda q: lambda e: e.activation(
                         out=sqscr2[:], in_=y2t[:, q, :], func=AF.Square,
                         accum_out=sq8[:, q:q+1]))(q),
                     [("ve", mv_add)], [("se", 1)])
            mv_s, ms_s = B.m("ve"), B.m("se")
            B.op("vector", lambda e: e.tensor_scalar(
                out=nmean8[:], in0=sum8[:], scalar1=-1.0/D, scalar2=None, op0=OP.mult),
                [("ve", mv_s)], [("ve", 1)])
            B.op("vector", lambda e: e.tensor_mul(tmp8[:], nmean8[:], nmean8[:]),
                 (), [("ve", 1)])
            B.op("vector", lambda e: e.tensor_scalar(
                out=var8[:], in0=sq8[:], scalar1=1.0/D, scalar2=None, op0=OP.mult),
                [("se", ms_s)], [("ve", 1)])
            B.op("vector", lambda e: e.tensor_sub(var8[:], var8[:], tmp8[:]),
                 (), [("ve", 1)])
            B.op("vector", lambda e: e.tensor_scalar(
                out=var8[:], in0=var8[:], scalar1=float(EPS), scalar2=None, op0=OP.add),
                (), [("ve", 1)])
            mv_v = B.m("ve")
            B.op("scalar", lambda e: e.activation(
                out=tmp8[:], in_=var8[:], func=AF.Ln),
                [("ve", mv_v)], [("se", 1)])
            B.op("scalar", lambda e: e.activation(
                out=rstd8[:], in_=tmp8[:], func=AF.Exp, scale=-0.5),
                (), [("se", 1)])
            ms_v = B.m("se")
            o_free = [0, 0]
            for q in range(NQT):
                sl = q % 2
                so = "so%d" % sl
                B.sem(so)
                B.op("vector",
                     (lambda q, sl: lambda e: e.tensor_scalar(
                         out=outs[:, sl, :], in0=y2t[:, q, :],
                         scalar1=nmean8[:, q:q+1], scalar2=rstd8[:, q:q+1],
                         op0=OP.add, op1=OP.mult))(q, sl),
                     [("se", ms_v), (so, o_free[sl])], [("ve", 1)])
                mv_o = B.m("ve")
                B.dma("sync",
                      (lambda q, sl: lambda e: e.dma_start(out=out_r[q], in_=outs[:, sl, :]))(q, sl),
                      so, [("ve", mv_o)])
                o_free[sl] = B.m(so)
            B.op("sync", None, [("so0", B.m("so0")), ("so1", B.m("so1"))], ())
            B.emit()

    return nc


_bf = ml_dtypes.bfloat16


def kernel(L0_attn_x, L0_attn_WQ, L0_attn_bQ, L0_attn_WK, L0_attn_bK,
           L0_attn_WV, L0_attn_bV, L0_attn_proj, L0_ffn_gate_w, L0_ffn_up_w,
           L0_ffn_down_w, L0_ffn_gate_s, L0_ffn_up_s, L0_ffn_down_s):
    f32 = np.float32
    x = np.asarray(L0_attn_x, f32)
    _wqT = np.asarray(L0_attn_WQ, f32).T
    wq4 = np.ascontiguousarray(
        np.stack([_wqT[:, r*DQ:(r+1)*DQ] for r in range(4)], axis=0))
    wkT = np.ascontiguousarray(np.asarray(L0_attn_WK, f32).T)
    wvT = np.ascontiguousarray(np.asarray(L0_attn_WV, f32).T)
    prj = np.ascontiguousarray(np.asarray(L0_attn_proj, f32))
    gT = np.ascontiguousarray(np.asarray(L0_ffn_gate_w, f32).T)
    uT = np.ascontiguousarray(np.asarray(L0_ffn_up_w, f32).T)
    dT = np.ascontiguousarray(np.asarray(L0_ffn_down_w, f32).T)
    bq = np.ascontiguousarray(np.asarray(L0_attn_bQ, f32).reshape(16, 128))
    bk = np.ascontiguousarray(np.asarray(L0_attn_bK, f32).reshape(4, 128))
    bv = np.ascontiguousarray(np.asarray(L0_attn_bV, f32).reshape(1, DQ))
    gs = np.ascontiguousarray(np.asarray(L0_ffn_gate_s, f32).reshape(IT, 128))
    us = np.ascontiguousarray(np.asarray(L0_ffn_up_s, f32).reshape(IT, 128))
    ds = np.ascontiguousarray(np.asarray(L0_ffn_down_s, f32).reshape(1, D))
    idf = np.eye(128, dtype=f32)
    ones2 = np.ones((1, DQ), dtype=_bf)

    in_maps = []
    for c in range(8):
        b, base = c // 4, (c % 4) * NQ
        xb = x[b]
        xqc = np.ascontiguousarray(xb[base:base + NQ])
        xTc = np.ascontiguousarray(np.roll(xb, -base, axis=0).T)
        tg = base + np.arange(NQ, dtype=np.int64)[:, None]
        sg = ((base + np.arange(T, dtype=np.int64)) % T)[None, :]
        maskc = np.where(sg > tg, f32(NEG), f32(0.0)).astype(_bf).reshape(NQT, 128, T)
        in_maps.append(dict(
            xq=xqc, xT=xTc, wq4=wq4, wkT=wkT, wvT=wvT, prj=prj, bqd=bq,
            bkd=bk, bvd=bv, gTd=gT, uTd=uT, dTd=dT, gsd=gs, usd=us, dsd=ds,
            msk=maskc, idfd=idf, onesd=ones2))

    nc = build_program()
    res = run_bass_kernel_spmd(nc, in_maps, core_ids=list(range(8)))
    outp = np.empty((2, T, D), dtype=f32)
    for c in range(8):
        b, base = c // 4, (c % 4) * NQ
        outp[b, base:base + NQ] = np.asarray(res.results[c]["outd"], dtype=f32)
    return outp
